# version 04
# baseline (speedup 1.0000x reference)
import numpy as np
import concourse.bass as bass
import concourse.mybir as mybir
from concourse.bass_utils import run_bass_kernel_spmd

F32 = mybir.dt.float32
BF16 = mybir.dt.bfloat16
ALU = mybir.AluOpType
AF = mybir.ActivationFunctionType
AX = mybir.AxisListType

T = 2048
D = 1024
NT = T // 128
KC = D // 128
INV_LN2 = float(np.float32(1.0 / np.log(2.0)))

WSHAPES = {
    "L0_attn_w_q": (D, D), "L0_attn_w_k": (D, D), "L0_attn_w_v": (D, D),
    "L0_attn_w_proj": (D, D), "L0_attn_b_proj": (1, D),
    "L0_attn_ln_gamma": (1, D), "L0_attn_ln_beta": (1, D),
    "L0_ffn_weight": (D, D), "L0_ffn_bias": (1, D),
    "L1_attn_Wq": (D, D), "L1_attn_Wk": (D, D), "L1_attn_Wv": (D, D),
    "L1_attn_Wo": (D, D), "L1_attn_Wb": (D, 16), "L1_attn_bb": (128, 16),
    "L1_ffn_w_gate": (2816, D), "L1_ffn_w_up": (2816, D),
    "L1_ffn_w_down": (D, 2816),
    "L2_attn_qkv_w": (3 * D, D), "L2_attn_qkv_b": (1, 3 * D),
    "L2_attn_proj_w": (D, D), "L2_attn_proj_b": (1, D),
    "L2_attn_rel_pos_emb": (2047, D),
    "L2_ffn_W1": (4096, D), "L2_ffn_b1": (1, 4096),
    "L2_ffn_W2": (D, 4096), "L2_ffn_b2": (1, D),
    "L3_attn_W_DQ": (512, 64), "L3_attn_W_UQ": (64, 512),
    "L3_attn_W_DKV": (1024, 64), "L3_attn_W_UKV": (128, 1024),
    "L3_attn_W_QR": (64, 64), "L3_attn_W_KR": (64, 64),
    "L3_attn_W_O": (64, 64),
    "L3_attn_q_norm_w": (1, 64), "L3_attn_k_norm_w": (1, 64),
    "L3_ffn_W1": (8 * 4096, D), "L3_ffn_W2": (8 * D, 4096),
    "L3_ffn_Wr": (8, D),
}


class Ctr:

    def __init__(self, nc, name):
        self.h = nc.alloc_semaphore(name)
        self.n = 0

    def inc(self, instr, k=1):
        instr.then_inc(self.h, k)
        self.n += k
        return self.n

    def wait(self, eng, at=None):
        at = self.n if at is None else at
        assert at <= self.n, f"wait on future count {at} > {self.n}"
        if at > 0:
            eng.wait_ge(self.h, at)


class KB:

    def __init__(self, n_sub=12):
        self.n_sub = n_sub
        nc = bass.Bass("TRN2", target_bir_lowering=False, debug=False)
        self.nc = nc
        self.t = nc.tensor
        self.v = nc.vector
        self.s = nc.scalar
        self.g = nc.gpsimd
        self.y = nc.sync

        self.din = {"x": nc.dram_tensor("x", [T, D], F32, kind="ExternalInput")}
        for k, shp in WSHAPES.items():
            self.din[k] = nc.dram_tensor(k, list(shp), F32, kind="ExternalInput")
        self.dout = nc.dram_tensor("out", [T, D], F32, kind="ExternalOutput")

        self.scr = {
            "l0w": nc.dram_tensor("scr_l0w", [D, D], BF16),
            "l1g": nc.dram_tensor("scr_l1g", [2816, D], BF16),
            "l1u": nc.dram_tensor("scr_l1u", [2816, D], BF16),
            "l1d": nc.dram_tensor("scr_l1d", [D, 2816], BF16),
            "l2qkv": nc.dram_tensor("scr_l2qkv", [3 * D, D], BF16),
            "l2proj": nc.dram_tensor("scr_l2proj", [D, D], BF16),
            "l2w1": nc.dram_tensor("scr_l2w1", [4096, D], BF16),
            "l2w2": nc.dram_tensor("scr_l2w2", [D, 4096], BF16),
            "l3w1": nc.dram_tensor("scr_l3w1", [8 * 4096, D], BF16),
            "l3w2": nc.dram_tensor("scr_l3w2", [8 * D, 4096], BF16),
            "l2bias": nc.dram_tensor("scr_l2bias", [17, 2304], F32),
            "cc_in": nc.dram_tensor("cc_in", [1, 8], F32),
            "cc_out": nc.dram_tensor("cc_out", [1, 8], F32),
        }

        self.cT = Ctr(nc, "cT")
        self.cV = Ctr(nc, "cV")
        self.cS = Ctr(nc, "cS")
        self.cG = Ctr(nc, "cG")
        self.qY = Ctr(nc, "qY")
        self.qG = Ctr(nc, "qG")
        self.cC = Ctr(nc, "cC")

        self.xr = nc.alloc_sbuf_tensor("xr", [128, NT * D], F32)
        self.stats = nc.alloc_sbuf_tensor("stats", [128, 256], F32)
        self.two = nc.alloc_sbuf_tensor("two", [128, 1], F32)
        self.wb = nc.alloc_sbuf_tensor("wb", [128, 1024], BF16)
        self.idn = nc.alloc_sbuf_tensor("idn", [128, 128], BF16)
        self.idn_f32 = nc.alloc_sbuf_tensor("idnf", [128, 128], F32)
        self.l0_sw = nc.alloc_sbuf_tensor("l0sw", [128, 2], F32)
        self.sqd = nc.alloc_sbuf_tensor("sqd", [128, D], F32)

    def vshort(self, ins):
        self.cV.inc(ins)
        self.v.wait_ge(self.cV.h, self.cV.n)
        return self.cV.n

    def gshort(self, ins):
        self.cG.inc(ins)
        self.g.wait_ge(self.cG.h, self.cG.n)
        return self.cG.n

    def fence_psum(self):
        self.cV.wait(self.t)
        self.cS.wait(self.t)

    def fence_dma(self, eng):
        self.cV.wait(eng)
        self.cS.wait(eng)
        self.cT.wait(eng)
        if eng is not self.g:
            self.cG.wait(eng)

    def emit_setup(self):
        src = self.din["x"].ap().rearrange("(n p) d -> p n d", p=128)
        dst = self.xr[:].rearrange("p (n d) -> p n d", d=D)
        self.xload = self.qY.inc(self.y.dma_start(out=dst, in_=src), 16)
        self.g.memset(self.two[:], 2.0)
        self.g.memset(self.idn[:], 1.0)
        self.cG.inc(self.g.affine_select(
            self.idn[:], self.idn[:], pattern=[[-1, 128]], base=0,
            channel_multiplier=1, compare_op=ALU.is_equal, fill=0.0))
        self.g.memset(self.idn_f32[:], 1.0)
        self.cG.inc(self.g.affine_select(
            self.idn_f32[:], self.idn_f32[:], pattern=[[-1, 128]], base=0,
            channel_multiplier=1, compare_op=ALU.is_equal, fill=0.0))
        self.cG.wait(self.t)
        self.cG.wait(self.v)

    def bcast_load(self, dst_ap, dram_name, cols, row=0, col0=0):
        src = self.din[dram_name].ap()[row:row + 1, col0:col0 + cols] \
            .partition_broadcast(128)
        return self.qY.inc(self.y.dma_start(out=dst_ap, in_=src), 16)

    def part_max(self, col_ap, out_row_ap, pm):
        self.cV.wait(self.t)
        self.cT.inc(self.t.transpose(pm[0:1, 0:128], col_ap, self.idn_f32[:]))
        self.cT.wait(self.v)
        self.vshort(self.v.tensor_reduce(out_row_ap, pm[0:1, 0:128],
                                         axis=AX.X, op=ALU.max))

    def emit_floor(self, eng, dst, src, levels, thr_tmp, bit_tmp, shift=0.0,
                   last_bit_out=None):
        c0 = float(levels[0])
        eng.tensor_scalar(bit_tmp, src, c0 - shift, None, op0=ALU.is_ge)
        eng.tensor_scalar(dst, bit_tmp, c0, None, op0=ALU.mult)
        for c in levels[1:]:
            c = float(c)
            eng.tensor_scalar(thr_tmp, dst, c - shift, None, op0=ALU.add)
            eng.tensor_tensor(bit_tmp, src, thr_tmp, op=ALU.is_ge)
            eng.scalar_tensor_tensor(dst, bit_tmp, c, dst, op0=ALU.mult,
                                     op1=ALU.add)
        if last_bit_out is not None:
            eng.tensor_copy(last_bit_out, bit_tmp)

    def emit_fp8_quant(self, src_f32, sscale_col, dst_bf16, tmp, W=512):
        v, g, s = self.v, self.g, self.s
        a = tmp[:, 0 * W:1 * W]
        m = tmp[:, 1 * W:2 * W]
        p1 = tmp[:, 2 * W:3 * W]
        p2 = tmp[:, 3 * W:4 * W]
        t1 = tmp[:, 4 * W:5 * W]
        t2 = tmp[:, 5 * W:6 * W]
        vv = tmp[:, 6 * W:7 * W]
        v.tensor_scalar(vv, src_f32, sscale_col, None, op0=ALU.mult)
        v.tensor_scalar(t2, vv, -1.0, None, op0=ALU.mult)
        v.tensor_tensor(a, vv, t2, op=ALU.max)
        self.cV.inc(v.tensor_scalar(a, a, 2.0 ** -12, None, op0=ALU.max))
        self.cV.wait(s)
        self.cS.inc(s.activation(t2, a, AF.Ln))
        self.cS.wait(v)
        v.tensor_scalar(p2, t2, INV_LN2, 6.0, op0=ALU.mult, op1=ALU.add)
        v.tensor_scalar(p2, p2, 0.0, 14.0, op0=ALU.max, op1=ALU.min)
        self.emit_floor(v, m, p2, [8, 4, 2, 1], thr_tmp=t1, bit_tmp=t2)
        v.tensor_scalar(t1, m, -1.0, 9.0, op0=ALU.mult, op1=ALU.add)
        self.cV.inc(v.tensor_scalar(t2, m, 1.0, -9.0, op0=ALU.mult, op1=ALU.add))
        self.cV.wait(g)
        g.tensor_tensor(p1, self.two[:, 0:1].broadcast_to([128, W]), t1, op=ALU.pow)
        self.cG.inc(g.tensor_tensor(p2, self.two[:, 0:1].broadcast_to([128, W]), t2, op=ALU.pow))
        self.cG.wait(v)
        v.tensor_tensor(a, a, p1, op=ALU.mult)
        self.emit_floor(v, m, a, [16, 8, 4, 2, 1], thr_tmp=t1, bit_tmp=t2,
                        shift=0.5, last_bit_out=p1)
        v.tensor_scalar(t1, a, 0.5, None, op0=ALU.add)
        v.tensor_tensor(t1, t1, m, op=ALU.is_equal)
        v.tensor_tensor(t1, t1, p1, op=ALU.mult)
        v.tensor_tensor(m, m, t1, op=ALU.subtract)
        self.cV.inc(v.tensor_tensor(m, m, p2, op=ALU.mult))
        self.cV.wait(s)
        self.cS.inc(s.activation(t1, vv, AF.Sign))
        self.cS.wait(v)
        self.cV.inc(v.tensor_tensor(dst_bf16, m, t1, op=ALU.mult))

    def emit_ln_stats(self, want_absmax=False, eps=1e-5):
        v, s = self.v, self.s
        st = self.stats
        self.cV.wait(s)
        self.qY.wait(s, self.xload)
        self.qY.wait(v, self.xload)
        for tt in range(NT):
            xs = self.xr[:, tt * D:(tt + 1) * D]
            ins = v.tensor_reduce(st[:, tt:tt + 1], xs, axis=AX.X, op=ALU.add)
            self.cS.inc(s.activation(self.sqd[:], xs, AF.Square,
                                     accum_out=st[:, 32 + tt:33 + tt]))
            if want_absmax:
                ins = v.tensor_reduce(st[:, 64 + tt:65 + tt], xs, axis=AX.X,
                                      op=ALU.max)
        self.vshort(ins)
        mu = st[:, 96:96 + NT]
        rr = st[:, 112:112 + NT]
        self.vshort(v.tensor_scalar(mu, st[:, 0:NT], 1.0 / D, None,
                                    op0=ALU.mult))
        self.vshort(v.tensor_tensor(rr, mu, mu, op=ALU.mult))
        self.cS.wait(v)
        self.vshort(v.scalar_tensor_tensor(rr, st[:, 32:32 + NT], 1.0 / D, rr,
                                           op0=ALU.mult, op1=ALU.subtract))
        self.cV.inc(v.tensor_scalar(rr, rr, eps, None, op0=ALU.add))
        self.cV.wait(s)
        self.cS.inc(s.activation(rr, rr, AF.Sqrt))
        self.cS.wait(v)
        self.vshort(v.reciprocal(rr, rr))
        return mu, rr

    def emit_ln_absmax_fix(self):
        v = self.v
        st = self.stats
        mu = st[:, 96:96 + NT]
        rr = st[:, 112:112 + NT]
        for tt in range(NT):
            xs = self.xr[:, tt * D:(tt + 1) * D]
            v.tensor_scalar(self.sqd[:], xs, -1.0, None, op0=ALU.mult)
            ins = v.tensor_reduce(st[:, 80 + tt:81 + tt], self.sqd[:],
                                  axis=AX.X, op=ALU.max)
        self.vshort(ins)
        am = st[:, 128:128 + NT]
        self.vshort(v.tensor_tensor(am, st[:, 64:64 + NT], mu,
                                    op=ALU.subtract))
        self.vshort(v.tensor_tensor(st[:, 80:80 + NT], st[:, 80:80 + NT], mu,
                                    op=ALU.add))
        self.vshort(v.tensor_tensor(am, am, st[:, 80:80 + NT], op=ALU.max))
        self.vshort(v.tensor_tensor(am, am, rr, op=ALU.mult))
        self.vshort(v.tensor_reduce(st[:, 145:146], am, axis=AX.X, op=ALU.max))

    def emit_xf(self, xf, lnbuf, psT, mu, rr, tiles=None, tk=T):
        v, t = self.v, self.t
        t_mark = {}
        v_mark = {}
        tiles = list(range(NT)) if tiles is None else list(tiles)
        for i_t, tt in enumerate(tiles):
            par = i_t % 2
            xs = self.xr[:, tt * D:(tt + 1) * D]
            ln = lnbuf[:, par * D:(par + 1) * D]
            if i_t >= 2:
                self.cT.wait(v, t_mark[i_t - 2])
            self.cV.inc(v.tensor_scalar(ln, xs, mu[:, tt:tt + 1],
                                        rr[:, tt:tt + 1],
                                        op0=ALU.subtract, op1=ALU.mult))
            self.cV.wait(t)
            if i_t >= 2:
                self.cV.wait(t, v_mark[i_t - 2])
            ins = None
            for c in range(KC):
                ins = t.transpose(
                    psT[:, par * D + c * 128:par * D + (c + 1) * 128],
                    ln[:, c * 128:(c + 1) * 128], self.idn[:])
            t_mark[i_t] = self.cT.inc(ins)
            self.cT.wait(v)
            dst = xf[:].rearrange("p (c tk) -> p c tk", tk=tk)[
                :, :, i_t * 128:(i_t + 1) * 128]
            srcp = psT[:, par * D:(par + 1) * D].rearrange(
                "p (c tk) -> p c tk", tk=128)
            v_mark[i_t] = self.cV.inc(v.tensor_copy(dst, srcp))
        return v_mark

    def emit_prep_l0w(self):
        nc = self.nc
        v, s, t, g, y = self.v, self.s, self.t, self.g, self.y
        with (nc.sbuf_tensor("p0_w", [128, KC * D], F32) as wsb,
              nc.sbuf_tensor("p0_tmp", [128, 7 * 512], F32) as tmp,
              nc.sbuf_tensor("p0_q", [128, D], BF16) as qtile,
              nc.psum_tensor("p0_pm", [128, 128], F32) as pm):
            mk = self.qY.inc(y.dma_start(
                out=wsb[:].rearrange("p (n d) -> p n d", d=D),
                in_=self.din["L0_ffn_weight"].ap().rearrange(
                    "(n p) d -> p n d", p=128)), 16)
            self.qY.wait(v, mk)
            st = self.stats
            for c in range(KC):
                ins = v.tensor_reduce(st[:, 160 + c:161 + c],
                                      wsb[:, c * D:(c + 1) * D],
                                      axis=AX.X, op=ALU.max,
                                      apply_absolute_value=True)
            self.vshort(ins)
            self.vshort(v.tensor_reduce(st[:, 168:169], st[:, 160:168],
                                        axis=AX.X, op=ALU.max))
            self.part_max(st[:, 168:169], st[0:1, 170:171], pm)
            self.cV.inc(v.tensor_copy(st[0:1, 171:179],
                                      st[0:1, 170:171].broadcast_to([1, 8])))
            self.cV.wait(y)
            self.qY.inc(y.dma_start(out=self.scr["cc_in"].ap()[0:1, :],
                                    in_=st[0:1, 171:179]), 16)
            self.qY.wait(y)
            mk2 = self.qY.inc(y.dma_start(
                out=st[:, 180:181],
                in_=self.scr["cc_in"].ap()[0:1, 0:1].partition_broadcast(128)),
                16)
            self.qY.wait(v, mk2)
            sw = st[:, 181:182]
            self.vshort(v.tensor_scalar(sw, st[:, 180:181], 1e-12, None,
                                        op0=ALU.max))
            self.vshort(v.tensor_scalar(self.l0_sw[:, 1:2], sw, 1.0 / 448.0,
                                        None, op0=ALU.mult))
            self.vshort(v.reciprocal(sw, sw))
            self.vshort(v.tensor_scalar(sw, sw, 448.0, None, op0=ALU.mult))
            for c in range(KC):
                for hh in range(2):
                    src = wsb[:, c * D + hh * 512: c * D + (hh + 1) * 512]
                    self.emit_fp8_quant(src, sw, qtile[:, hh * 512:(hh + 1) * 512],
                                        tmp, W=512)
                self.cV.wait(y)
                self.qY.inc(y.dma_start(
                    out=self.scr["l0w"].ap()[c * 128:(c + 1) * 128, :],
                    in_=qtile[:]), 16)
                self.qY.wait(v)

    def emit_l0_ffn(self, tag):
        nc = self.nc
        v, s, t, g, y = self.v, self.s, self.t, self.g, self.y
        st = self.stats
        mu, rr = self.emit_ln_stats(want_absmax=True)
        self.emit_ln_absmax_fix()
        with (nc.sbuf_tensor(f"{tag}_tmp", [128, 8 * 512], F32) as tmp,
              nc.sbuf_tensor(f"{tag}_cb", [128, D], F32) as cbl,
              nc.sbuf_tensor(f"{tag}_xq", [128, 2 * D], BF16) as xqbuf,
              nc.sbuf_tensor(f"{tag}_xqT", [128, KC * T], BF16) as xqT,
              nc.sbuf_tensor(f"{tag}_qw", [128, KC * D], BF16) as qwT,
              nc.psum_tensor(f"{tag}_po", [128, 2 * 512], F32) as pOut,
              nc.psum_tensor(f"{tag}_pm", [128, 128], F32) as pm,
              nc.psum_tensor(f"{tag}_pq", [128, 2 * D], BF16) as psQ):
            self.fence_psum()
            self.part_max(st[:, 145:146], st[0:1, 146:147], pm)
            self.cV.inc(v.tensor_copy(st[0:1, 147:155],
                                      st[0:1, 146:147].broadcast_to([1, 8])))
            self.cV.wait(y)
            self.qY.inc(y.dma_start(out=self.scr["cc_in"].ap()[0:1, :],
                                    in_=st[0:1, 147:155]), 16)
            self.qY.wait(g)
            self.cC.inc(g.collective_compute(
                "AllReduce", ALU.max, replica_groups=[list(range(8))],
                ins=[self.scr["cc_in"].ap().opt()],
                outs=[self.scr["cc_out"].ap().opt()]))
            self.cC.wait(y)
            gm = self.qY.inc(y.dma_start(
                out=st[:, 148:149],
                in_=self.scr["cc_out"].ap()[0:1, 0:1].partition_broadcast(128)),
                16)
            self.qY.wait(v, gm)
            sx = st[:, 149:150]
            cinv = st[:, 150:151]
            self.vshort(v.tensor_scalar(sx, st[:, 148:149], 1e-12, None,
                                        op0=ALU.max))
            self.vshort(v.tensor_scalar(cinv, sx, 1.0 / 448.0, None,
                                        op0=ALU.mult))
            self.vshort(v.tensor_tensor(cinv, cinv, self.l0_sw[:, 1:2],
                                        op=ALU.mult))
            self.vshort(v.reciprocal(sx, sx))
            self.vshort(v.tensor_scalar(sx, sx, 448.0, None, op0=ALU.mult))
            self.fence_dma(y)
            for kc in range(KC):
                src = self.scr["l0w"].ap()[0:D, kc * 128:(kc + 1) * 128]
                self.qY.inc(y.dma_start_transpose(
                    qwT[:, kc * D:(kc + 1) * D], src), 16)
            wmark = self.qY.n
            self.cV.wait(y)
            bmark = self.bcast_load(cbl[:, 0:D], "L0_ffn_bias", D)
            self.fence_psum()
            vq_mark = {}
            tq_mark = {}
            for tt in range(NT):
                xs = self.xr[:, tt * D:(tt + 1) * D]
                par = tt % 2
                xq = xqbuf[:, par * D:(par + 1) * D]
                if tt >= 2:
                    self.cT.wait(v, tq_mark[tt - 2])
                for hh in range(2):
                    lnf = tmp[:, 7 * 512:8 * 512]
                    self.cV.inc(v.tensor_scalar(
                        lnf, xs[:, hh * 512:(hh + 1) * 512], mu[:, tt:tt + 1],
                        rr[:, tt:tt + 1], op0=ALU.subtract, op1=ALU.mult))
                    self.emit_fp8_quant(lnf, sx, xq[:, hh * 512:(hh + 1) * 512],
                                        tmp, W=512)
                self.cV.wait(t)
                if tt >= 2:
                    self.cV.wait(t, vq_mark[tt - 2])
                ins = None
                for c in range(KC):
                    ins = t.transpose(
                        psQ[:, par * D + c * 128:par * D + (c + 1) * 128],
                        xq[:, c * 128:(c + 1) * 128], self.idn[:])
                tq_mark[tt] = self.cT.inc(ins)
                self.cT.wait(v)
                dst = xqT[:].rearrange("p (c tk) -> p c tk", tk=T)[
                    :, :, tt * 128:(tt + 1) * 128]
                srcp = psQ[:, par * D:(par + 1) * D].rearrange(
                    "p (c tk) -> p c tk", tk=128)
                vq_mark[tt] = self.cV.inc(v.tensor_copy(dst, srcp))
            if getattr(self, "dbg", False):
                self.qY.wait(y, wmark)
                self.cV.wait(y)
                self.qY.inc(y.dma_start(out=self.ddbg2.ap()[:, :],
                                        in_=qwT[:, 0:2048]), 16)
                self.qY.inc(y.dma_start(out=self.ddbg3.ap()[:, :],
                                        in_=xqT[:, 0:2048]), 16)
            self.qY.wait(t, wmark)
            self.qY.wait(v, bmark)
            vo_mark = {}
            for tt in range(NT):
                self.cV.wait(t, vq_mark[tt])
                for og in range(2):
                    pslot = pOut[:, og * 512:(og + 1) * 512]
                    if tt >= 1:
                        self.cV.wait(t, vo_mark[(tt - 1, og)])
                    ins = None
                    for kc in range(KC):
                        ins = t.matmul(
                            pslot,
                            xqT[:, kc * T + tt * 128: kc * T + (tt + 1) * 128],
                            qwT[:, kc * D + og * 512: kc * D + (og + 1) * 512],
                            start=(kc == 0), stop=(kc == KC - 1))
                    mk = self.cT.inc(ins)
                    self.cT.wait(v, mk)
                    ots = tmp[:, 0:512]
                    v.scalar_tensor_tensor(ots, pslot, cinv,
                                           cbl[:, og * 512:(og + 1) * 512],
                                           op0=ALU.mult, op1=ALU.add)
                    xslice = self.xr[:, tt * D + og * 512:
                                     tt * D + (og + 1) * 512]
                    vo_mark[(tt, og)] = self.cV.inc(
                        v.scalar_tensor_tensor(xslice, ots, 0.5, xslice,
                                               op0=ALU.mult, op1=ALU.add))

    def emit_l0_attn(self, tag):
        nc = self.nc
        v, s, t, g, y = self.v, self.s, self.t, self.g, self.y
        st = self.stats
        H, dh = 16, 64
        mu, rr = self.emit_ln_stats()
        with (nc.sbuf_tensor(f"{tag}_xo", [128, KC * T], BF16) as xo,
              nc.sbuf_tensor(f"{tag}_kT", [128, KC * T], BF16) as kT):
            with (nc.sbuf_tensor(f"{tag}_ln", [128, 2 * D], BF16) as lnbuf,
                  nc.psum_tensor(f"{tag}_psT", [128, 2 * D], BF16) as psT):
                self.emit_xf(xo, lnbuf, psT, mu, rr)
            xf_done_v = self.cV.n
            with (nc.sbuf_tensor(f"{tag}_qT", [128, KC * T], BF16) as qT,
                  nc.sbuf_tensor(f"{tag}_vt", [128, NT * D], BF16) as vt,
                  nc.sbuf_tensor(f"{tag}_pt", [128, 2048], BF16) as ptsb,
                  nc.sbuf_tensor(f"{tag}_pb", [128, 2048], BF16) as pbsb):
                self.fence_dma(g)
                for kc in range(KC):
                    for og in range(2):
                        self.qG.inc(g.dma_start(
                            out=kT[:, (kc * 2 + og) * 512:
                                   (kc * 2 + og + 1) * 512],
                            in_=self.din["L0_attn_w_v"].ap()[
                                kc * 128:(kc + 1) * 128,
                                og * 512:(og + 1) * 512]), 16)
                self.qG.wait(t)
                self.fence_psum()
                self.cV.wait(t, xf_done_v)
                with nc.psum_tensor(f"{tag}_pv", [128, 1024], F32) as psV:
                    ev = {}
                    for tt in range(NT):
                        for og in range(2):
                            if tt >= 1:
                                self.cV.wait(t, ev[(tt - 1, og)])
                            ins = None
                            for kc in range(KC):
                                ins = t.matmul(
                                    psV[:, og * 512:(og + 1) * 512],
                                    xo[:, kc * T + tt * 128:
                                       kc * T + (tt + 1) * 128],
                                    kT[:, (kc * 2 + og) * 512:
                                       (kc * 2 + og + 1) * 512],
                                    start=(kc == 0), stop=(kc == KC - 1))
                            mk = self.cT.inc(ins)
                            self.cT.wait(v, mk)
                            ev[(tt, og)] = self.cV.inc(v.tensor_copy(
                                vt[:, tt * D + og * 512:
                                   tt * D + (og + 1) * 512],
                                psV[:, og * 512:(og + 1) * 512]))
                v_done_t = self.cT.n
                if getattr(self, "stop_phase", "") == "V":
                    return
                with nc.psum_tensor(f"{tag}_pq", [128, 1024], F32) as psQK:
                    evp = [0, 0]
                    gi = 0
                    wb_user_t = v_done_t
                    for qq, (W, dst, scale) in enumerate(
                            (("L0_attn_w_q", qT, 0.125),
                             ("L0_attn_w_k", kT, 1.0))):
                        for dc in range(KC):
                            self.cT.wait(g, wb_user_t)
                            wm = None
                            for kc in range(KC):
                                wm = self.qG.inc(g.dma_start(
                                    out=self.wb[:, kc * 128:(kc + 1) * 128],
                                    in_=self.din[W].ap()[
                                        kc * 128:(kc + 1) * 128,
                                        dc * 128:(dc + 1) * 128]), 16)
                            self.qG.wait(t, wm)
                            for tg in range(4):
                                par = gi % 2
                                gi += 1
                                self.cV.wait(t, evp[par])
                                ins = None
                                for kc in range(KC):
                                    ins = t.matmul(
                                        psQK[:, par * 512:(par + 1) * 512],
                                        self.wb[:, kc * 128:(kc + 1) * 128],
                                        xo[:, kc * T + tg * 512:
                                           kc * T + (tg + 1) * 512],
                                        start=(kc == 0), stop=(kc == KC - 1))
                                mk = self.cT.inc(ins)
                                self.cT.wait(v, mk)
                                evp[par] = self.cV.inc(v.tensor_scalar(
                                    dst[:, dc * T + tg * 512:
                                        dc * T + (tg + 1) * 512],
                                    psQK[:, par * 512:(par + 1) * 512],
                                    scale, None, op0=ALU.mult))
                            wb_user_t = self.cT.n
                qkv_done_v = self.cV.n
                if getattr(self, "stop_phase", "") == "QK":
                    return
                with (nc.psum_tensor(f"{tag}_psS", [128, 2048], F32) as psS,
                      nc.psum_tensor(f"{tag}_psP", [128, 2048], BF16) as psP,
                      nc.psum_tensor(f"{tag}_psO", [128, 64], F32) as psO):
                    self.fence_psum()
                    exp_mark = 0
                    pt_mark = 0
                    evp_mark = 0
                    evo_mark = 0
                    self.cV.wait(t, qkv_done_v)
                    for h in range(H):
                        cbase = (h // 2) * T
                        pbase = (h % 2) * 64
                        for qt in range(NT):
                            self.cS.wait(t, exp_mark)
                            ins = None
                            for kg in range(4):
                                ins = t.matmul(
                                    psS[:, kg * 512:(kg + 1) * 512],
                                    qT[pbase:pbase + 64,
                                       cbase + qt * 128:cbase + (qt + 1) * 128],
                                    kT[pbase:pbase + 64,
                                       cbase + kg * 512:cbase + (kg + 1) * 512],
                                    start=True, stop=True)
                            smk = self.cT.inc(ins)
                            self.cT.wait(v, smk)
                            self.vshort(v.tensor_reduce(
                                st[:, 200:201], psS[:], axis=AX.X,
                                op=ALU.max, negate=True))
                            self.cV.wait(s)
                            self.cT.wait(s, pt_mark)
                            exp_mark = self.cS.inc(s.activation(
                                pbsb[:], psS[:], AF.Exp, bias=st[:, 200:201],
                                scale=1.0, accum_out=st[:, 201:202]))
                            self.cS.wait(v)
                            self.vshort(v.reciprocal(st[:, 202:203],
                                                     st[:, 201:202]))
                            self.cV.wait(t, evp_mark)
                            self.cS.wait(t)
                            ins = None
                            for kc in range(NT):
                                ins = t.transpose(
                                    psP[:, kc * 128:(kc + 1) * 128],
                                    pbsb[:, kc * 128:(kc + 1) * 128],
                                    self.idn[:])
                            pt_mark = self.cT.inc(ins)
                            self.cT.wait(v, pt_mark)
                            evp_mark = self.cV.inc(
                                v.tensor_copy(ptsb[:], psP[:]))
                            self.cV.wait(t, evp_mark)
                            self.cV.wait(t, evo_mark)
                            ins = None
                            for kc in range(NT):
                                ins = t.matmul(
                                    psO[:],
                                    ptsb[:, kc * 128:(kc + 1) * 128],
                                    vt[:, kc * D + h * dh:
                                       kc * D + (h + 1) * dh],
                                    start=(kc == 0), stop=(kc == NT - 1))
                            omk = self.cT.inc(ins)
                            self.cT.wait(v, omk)
                            col = (h // 2) * T + qt * 128 + (h % 2) * 64
                            evo_mark = self.cV.inc(v.tensor_scalar(
                                xo[:, col:col + 64],
                                psO[:], st[:, 202:203], None, op0=ALU.mult))
                o_done_v = self.cV.n
                if getattr(self, "stop_phase", "") == "SC":
                    return
                with nc.psum_tensor(f"{tag}_psOT", [128, 2 * D], BF16) as psOT:
                    self.fence_psum()
                    self.cV.wait(t, o_done_v)
                    evq = {}
                    for qt in range(NT):
                        par = qt % 2
                        if qt >= 2:
                            self.cV.wait(t, evq[qt - 2])
                        ins = None
                        for hp in range(KC):
                            ins = t.transpose(
                                psOT[:, par * D + hp * 128:
                                     par * D + (hp + 1) * 128],
                                xo[:, hp * T + qt * 128:hp * T + (qt + 1) * 128],
                                self.idn[:])
                        tmk = self.cT.inc(ins)
                        self.cT.wait(v, tmk)
                        dst = kT[:].rearrange("p (c tk) -> p c tk", tk=T)[
                            :, :, qt * 128:(qt + 1) * 128]
                        srcp = psOT[:, par * D:(par + 1) * D].rearrange(
                            "p (c tk) -> p c tk", tk=128)
                        evq[qt] = self.cV.inc(v.tensor_copy(dst, srcp))
            oT_done_v = self.cV.n
            if getattr(self, "stop_phase", "") == "OT":
                return
            with (nc.sbuf_tensor(f"{tag}_wpj", [128, 8192], BF16) as wpj,
                  nc.sbuf_tensor(f"{tag}_v2", [128, D], F32) as v2,
                  nc.sbuf_tensor(f"{tag}_sq2", [128, D], F32) as sq2,
                  nc.sbuf_tensor(f"{tag}_cb", [128, 3 * D], F32) as cbl,
                  nc.psum_tensor(f"{tag}_psPr", [128, 1024], F32) as psPr):
                self.fence_dma(y)
                self.fence_dma(g)
                b1 = self.bcast_load(cbl[:, 0:D], "L0_attn_b_proj", D)
                b2 = self.bcast_load(cbl[:, D:2 * D], "L0_attn_ln_gamma", D)
                srcb = self.din["L0_attn_ln_beta"].ap()[0:1, 0:D] \
                    .partition_broadcast(128)
                b3 = self.qY.inc(y.dma_start(out=cbl[:, 2 * D:3 * D],
                                             in_=srcb), 16)
                for og in range(2):
                    for dc in range(KC):
                        wm = self.qG.inc(g.dma_start(
                            out=wpj[:, og * 4096 + dc * 512:
                                    og * 4096 + (dc + 1) * 512],
                            in_=self.din["L0_attn_w_proj"].ap()[
                                dc * 128:(dc + 1) * 128,
                                og * 512:(og + 1) * 512]), 16)
                self.qY.wait(v, b3)
                self.qG.wait(t, wm)
                self.fence_psum()
                self.cV.wait(t, oT_done_v)
                evx = {}
                for tt in range(NT):
                    for og in range(2):
                        if tt >= 1:
                            self.cV.wait(t, evx[(tt - 1, og)])
                        ins = None
                        for dc in range(KC):
                            ins = t.matmul(
                                psPr[:, og * 512:(og + 1) * 512],
                                kT[:].rearrange("p (c tk) -> p c tk", tk=T)[
                                    :, dc, tt * 128:(tt + 1) * 128],
                                wpj[:, og * 4096 + dc * 512:
                                    og * 4096 + (dc + 1) * 512],
                                start=(dc == 0), stop=(dc == KC - 1))
                        mk = self.cT.inc(ins)
                        self.cT.wait(v, mk)
                        xsl = self.xr[:, tt * D + og * 512:
                                      tt * D + (og + 1) * 512]
                        v.tensor_scalar(
                            v2[:, og * 512:(og + 1) * 512], xsl,
                            mu[:, tt:tt + 1], rr[:, tt:tt + 1],
                            op0=ALU.subtract, op1=ALU.mult)
                        v.tensor_tensor(v2[:, og * 512:(og + 1) * 512],
                                        v2[:, og * 512:(og + 1) * 512],
                                        psPr[:, og * 512:(og + 1) * 512],
                                        op=ALU.add)
                        evx[(tt, og)] = self.cV.inc(v.tensor_tensor(
                            v2[:, og * 512:(og + 1) * 512],
                            v2[:, og * 512:(og + 1) * 512],
                            cbl[:, og * 512:(og + 1) * 512], op=ALU.add))
                    v.tensor_reduce(st[:, 204:205], v2[:], axis=AX.X,
                                    op=ALU.add)
                    v.tensor_tensor(sq2[:], v2[:], v2[:], op=ALU.mult)
                    v.tensor_reduce(st[:, 205:206], sq2[:], axis=AX.X,
                                    op=ALU.add)
                    self.vshort(v.tensor_scalar(st[:, 206:207],
                                                st[:, 204:205],
                                                1.0 / D, None, op0=ALU.mult))
                    self.vshort(v.tensor_tensor(st[:, 207:208],
                                                st[:, 206:207],
                                                st[:, 206:207], op=ALU.mult))
                    self.vshort(v.scalar_tensor_tensor(
                        st[:, 207:208], st[:, 205:206], 1.0 / D,
                        st[:, 207:208], op0=ALU.mult, op1=ALU.subtract))
                    self.cV.inc(v.tensor_scalar(st[:, 207:208],
                                                st[:, 207:208],
                                                1e-6, None, op0=ALU.add))
                    self.cV.wait(s)
                    self.cS.inc(s.activation(st[:, 207:208], st[:, 207:208],
                                             AF.Sqrt))
                    self.cS.wait(v)
                    self.vshort(v.reciprocal(st[:, 207:208], st[:, 207:208]))
                    if getattr(self, "dbg_proj", False) and tt == 0:
                        self.cV.wait(y)
                        self.qY.inc(y.dma_start(
                            out=self.dout.ap()[0:128, :], in_=v2[:]), 16)
                        self.qY.inc(y.dma_start(
                            out=self.dout.ap()[128:256, :],
                            in_=cbl[:, 0:D]), 16)
                        self.qY.inc(y.dma_start(
                            out=self.dout.ap()[256:384, :],
                            in_=cbl[:, D:2 * D]), 16)
                        self.qY.inc(y.dma_start(
                            out=self.dout.ap()[384:512, :],
                            in_=cbl[:, 2 * D:3 * D]), 16)
                        self.qY.inc(y.dma_start(
                            out=self.dout.ap()[512:640, 0:256],
                            in_=st[:, 0:256]), 16)
                        self.qY.wait(y)
                    for og in range(2):
                        sl = slice(og * 512, (og + 1) * 512)
                        xsl = self.xr[:, tt * D + og * 512:
                                      tt * D + (og + 1) * 512]
                        v.tensor_scalar(v2[:, sl], v2[:, sl],
                                        st[:, 206:207], st[:, 207:208],
                                        op0=ALU.subtract, op1=ALU.mult)
                        v.tensor_tensor(v2[:, sl], v2[:, sl],
                                        cbl[:, D + og * 512:D + (og + 1) * 512],
                                        op=ALU.mult)
                        v.tensor_tensor(v2[:, sl], v2[:, sl],
                                        cbl[:, 2 * D + og * 512:
                                            2 * D + (og + 1) * 512],
                                        op=ALU.add)
                        self.cV.inc(v.tensor_tensor(xsl, xsl, v2[:, sl],
                                                    op=ALU.add))

    def emit_fp4_quant(self, eng, src_f32, dst_bf16, tmp, sc_cols, W):
        NB = W // 16
        a = tmp[:, 0 * W:1 * W]
        t1 = tmp[:, 1 * W:2 * W]
        t2 = tmp[:, 2 * W:3 * W]
        bm = sc_cols[:, 0:NB]
        rs = sc_cols[:, NB:2 * NB]
        eng.tensor_scalar(t1, src_f32, -1.0, None, op0=ALU.mult)
        eng.tensor_tensor(a, src_f32, t1, op=ALU.max)
        ins = eng.tensor_reduce(bm, a.rearrange("p (g d) -> p g d", d=16),
                                axis=AX.X, op=ALU.max)
        self._short(eng, ins)
        ins = eng.tensor_scalar(bm, bm, 1.0 / 6.0, 1e-12, op0=ALU.mult,
                                op1=ALU.max)
        self._short(eng, ins)
        ins = eng.reciprocal(rs, bm) if eng is self.v else \
            eng.tensor_scalar(rs, bm, 1.0, None, op0=ALU.divide)
        self._short(eng, ins)
        bc = rs.unsqueeze(2).broadcast_to([128, NB, 16])
        a3 = a.rearrange("p (g d) -> p g d", d=16)
        eng.tensor_tensor(a3, a3, bc, op=ALU.mult)
        eng.tensor_scalar(t1, a, 0.25, None, op0=ALU.is_gt)
        eng.scalar_tensor_tensor(t1, a, 0.75, t1, op0=ALU.is_gt, op1=ALU.add)
        eng.scalar_tensor_tensor(t1, a, 1.25, t1, op0=ALU.is_gt, op1=ALU.add)
        eng.scalar_tensor_tensor(t1, a, 1.75, t1, op0=ALU.is_gt, op1=ALU.add)
        eng.tensor_scalar(t2, a, 2.5, None, op0=ALU.is_gt)
        eng.scalar_tensor_tensor(t2, a, 3.5, t2, op0=ALU.is_gt, op1=ALU.add)
        eng.scalar_tensor_tensor(t1, t1, 0.5, t2, op0=ALU.mult, op1=ALU.add)
        eng.tensor_scalar(t2, a, 5.0, 2.0, op0=ALU.is_gt, op1=ALU.mult)
        eng.tensor_tensor(t1, t1, t2, op=ALU.add)
        eng.tensor_scalar(t2, src_f32, 0.0, None, op0=ALU.is_gt)
        eng.tensor_scalar(a, src_f32, 0.0, None, op0=ALU.is_lt)
        eng.tensor_tensor(t2, t2, a, op=ALU.subtract)
        eng.tensor_tensor(t1, t1, t2, op=ALU.mult)
        t13 = t1.rearrange("p (g d) -> p g d", d=16)
        bmc = bm.unsqueeze(2).broadcast_to([128, NB, 16])
        return eng.tensor_tensor(dst_bf16.rearrange("p (g d) -> p g d", d=16),
                                 t13, bmc, op=ALU.mult)

    def _short(self, eng, ins):
        if eng is self.v:
            self.cV.inc(ins)
            self.v.wait_ge(self.cV.h, self.cV.n)
        else:
            self.cG.inc(ins)
            self.g.wait_ge(self.cG.h, self.cG.n)

    def emit_prep_l1(self):
        nc = self.nc
        v, g, y = self.v, self.g, self.y
        with (nc.sbuf_tensor("p1_w", [128, 2816], F32) as wsb,
              nc.sbuf_tensor("p1_tmp", [128, 3 * 2816], F32) as tmp,
              nc.sbuf_tensor("p1_sc", [128, 512], F32) as sc,
              nc.sbuf_tensor("p1_q", [128, 2816], BF16) as qtile):
            for name, scr, nrows, ncols in (
                    ("L1_ffn_w_gate", "l1g", 2816, D),
                    ("L1_ffn_w_up", "l1u", 2816, D),
                    ("L1_ffn_w_down", "l1d", D, 2816)):
                ntile = nrows // 128
                for r in range(ntile):
                    self.fence_dma(y)
                    mk = self.qY.inc(y.dma_start(
                        out=wsb[:, 0:ncols],
                        in_=self.din[name].ap()[r * 128:(r + 1) * 128, :]), 16)
                    self.qY.wait(v, mk)
                    ins = self.emit_fp4_quant(v, wsb[:, 0:ncols],
                                              qtile[:, 0:ncols], tmp, sc,
                                              W=ncols)
                    self.cV.inc(ins)
                    self.cV.wait(y)
                    self.qY.inc(y.dma_start(
                        out=self.scr[scr].ap()[r * 128:(r + 1) * 128, :],
                        in_=qtile[:, 0:ncols]), 16)
                    self.qY.wait(v)

    def emit_l1_ffn(self, tag):
        nc = self.nc
        v, s, t, g, y = self.v, self.s, self.t, self.g, self.y
        NI = 2816
        NC22 = NI // 128
        TG = 1024
        NTG = 8
        mu, rr = self.emit_ln_stats()
        with (nc.sbuf_tensor(f"{tag}_xqT", [128, KC * TG], BF16) as xqT,
              nc.sbuf_tensor(f"{tag}_hqT", [128, NC22 * TG], BF16) as hqT):
            for tgrp in range(2):
                tbase = tgrp * NTG
                with (nc.sbuf_tensor(f"{tag}{tgrp}_tmp", [128, 3 * D], F32) as tmp,
                      nc.sbuf_tensor(f"{tag}{tgrp}_lnf", [128, D], F32) as lnf,
                      nc.sbuf_tensor(f"{tag}{tgrp}_sc", [128, 128], F32) as sc,
                      nc.sbuf_tensor(f"{tag}{tgrp}_xq", [128, 2 * D], BF16) as xqbuf,
                      nc.psum_tensor(f"{tag}{tgrp}_pq", [128, 2 * D], BF16) as psQ):
                    self.fence_psum()
                    vq_mark = {}
                    tq_mark = {}
                    for it in range(NTG):
                        tt = tbase + it
                        xs = self.xr[:, tt * D:(tt + 1) * D]
                        par = it % 2
                        xq = xqbuf[:, par * D:(par + 1) * D]
                        if it >= 2:
                            self.cT.wait(v, tq_mark[it - 2])
                        v.tensor_scalar(lnf[:], xs, mu[:, tt:tt + 1],
                                        rr[:, tt:tt + 1],
                                        op0=ALU.subtract, op1=ALU.mult)
                        self.cV.inc(self.emit_fp4_quant(v, lnf[:], xq, tmp,
                                                        sc, W=D))
                        self.cV.wait(t)
                        if it >= 2:
                            self.cV.wait(t, vq_mark[it - 2])
                        ins = None
                        for c in range(KC):
                            ins = t.transpose(
                                psQ[:, par * D + c * 128:
                                    par * D + (c + 1) * 128],
                                xq[:, c * 128:(c + 1) * 128], self.idn[:])
                        tq_mark[it] = self.cT.inc(ins)
                        self.cT.wait(v)
                        dst = xqT[:].rearrange("p (c tk) -> p c tk", tk=TG)[
                            :, :, it * 128:(it + 1) * 128]
                        srcp = psQ[:, par * D:(par + 1) * D].rearrange(
                            "p (c tk) -> p c tk", tk=128)
                        vq_mark[it] = self.cV.inc(v.tensor_copy(dst, srcp))
                    xq_done_v = self.cV.n
                GW = 256
                NG = NI // GW
                with (nc.sbuf_tensor(f"{tag}{tgrp}_wg", [128, KC * GW], BF16) as wg,
                      nc.sbuf_tensor(f"{tag}{tgrp}_wu", [128, KC * GW], BF16) as wu,
                      nc.sbuf_tensor(f"{tag}{tgrp}_tmp2", [128, 5 * GW], F32) as tmp2,
                      nc.sbuf_tensor(f"{tag}{tgrp}_sc2", [128, 32], F32) as sc2,
                      nc.sbuf_tensor(f"{tag}{tgrp}_hq", [128, 2 * GW], BF16) as hqb,
                      nc.psum_tensor(f"{tag}{tgrp}_pg", [128, 512], F32) as psG,
                      nc.psum_tensor(f"{tag}{tgrp}_pu", [128, 512], F32) as psU,
                      nc.psum_tensor(f"{tag}{tgrp}_ph", [128, 2 * 1024], BF16) as psH):
                    hf = tmp2[:, 3 * GW:4 * GW]
                    sgm = tmp2[:, 4 * GW:5 * GW]
                    self.fence_psum()
                    self.cV.wait(t, xq_done_v)
                    thq = {}
                    gi = 0
                    for ng in range(NG):
                        n0 = ng * GW
                        if ng == 0 and tgrp == 0:
                            self.fence_dma(y)
                        self.cT.wait(y)
                        wm = None
                        for kc in range(KC):
                            wm = self.qY.inc(y.dma_start_transpose(
                                wg[:, kc * GW:(kc + 1) * GW],
                                self.scr["l1g"].ap()[
                                    n0:n0 + GW, kc * 128:(kc + 1) * 128]), 16)
                            wm = self.qY.inc(y.dma_start_transpose(
                                wu[:, kc * GW:(kc + 1) * GW],
                                self.scr["l1u"].ap()[
                                    n0:n0 + GW, kc * 128:(kc + 1) * 128]), 16)
                        self.qY.wait(t, wm)
                        for it in range(NTG):
                            gi += 1
                            ins = None
                            for kc in range(KC):
                                ins = t.matmul(
                                    psG[:, 0:GW],
                                    xqT[:, kc * TG + it * 128:
                                        kc * TG + (it + 1) * 128],
                                    wg[:, kc * GW:(kc + 1) * GW],
                                    start=(kc == 0), stop=(kc == KC - 1))
                            for kc in range(KC):
                                ins = t.matmul(
                                    psU[:, 0:GW],
                                    xqT[:, kc * TG + it * 128:
                                        kc * TG + (it + 1) * 128],
                                    wu[:, kc * GW:(kc + 1) * GW],
                                    start=(kc == 0), stop=(kc == KC - 1))
                            mk = self.cT.inc(ins)
                            self.cT.wait(s, mk)
                            self.cS.inc(s.activation(sgm, psG[:, 0:GW],
                                                     AF.Silu))
                            self.cS.wait(v)
                            v.tensor_tensor(hf, sgm, psU[:, 0:GW],
                                            op=ALU.mult)
                            par = gi % 2
                            hq = hqb[:, par * GW:(par + 1) * GW]
                            psl = par * 1024
                            self.cV.inc(self.emit_fp4_quant(
                                v, hf, hq, tmp2, sc2, W=GW))
                            if getattr(self, "dbg2", False) and tgrp == 0 \
                                    and gi == 1:
                                self.cV.wait(y)
                                self.qY.inc(y.dma_start(
                                    out=self.dq2.ap()[:, 2048:2048 + GW],
                                    in_=hq), 16)
                                self.qY.wait(v)
                            self.cV.wait(t)
                            if gi > 2:
                                self.cV.wait(t, thq.get(gi - 2, 0))
                            ins = None
                            nch = GW // 128
                            for c in range(nch):
                                ins = t.transpose(
                                    psH[:, psl + c * 128:psl + (c + 1) * 128],
                                    hq[:, c * 128:(c + 1) * 128], self.idn[:])
                            tm = self.cT.inc(ins)
                            self.cT.wait(v, tm)
                            dst = hqT[:].rearrange(
                                "p (c tk) -> p c tk", tk=TG)[
                                :, 2 * ng:2 * ng + nch,
                                it * 128:(it + 1) * 128]
                            srcp = psH[:, psl:psl + GW].rearrange(
                                "p (c tk) -> p c tk", tk=128)
                            thq[gi] = self.cV.inc(v.tensor_copy(dst, srcp))
                            self.cS.wait(t)
                            self.cV.wait(t)
                    hq_done_v = self.cV.n
                    if getattr(self, "dbg2", False) and tgrp == 0:
                        self.cV.wait(y)
                        self.qY.inc(y.dma_start(out=self.dq1.ap()[:, :],
                                                in_=hqT[:, 0:4096]), 16)
                        self.qY.inc(y.dma_start(out=self.dq2.ap()[:, 0:2048],
                                                in_=xqT[:, 0:2048]), 16)
                        self.qY.inc(y.dma_start(
                            out=self.dq3.ap()[:, :],
                            in_=self.scr["l1g"].ap()[0:128, :]), 16)
                        self.qY.wait(y)
                        self.qY.wait(v)
                        self.qY.wait(t)
                with (nc.sbuf_tensor(f"{tag}{tgrp}_wd", [128, NC22 * 512], BF16) as wd,
                      nc.psum_tensor(f"{tag}{tgrp}_pd", [128, 2 * 512], F32) as psD):
                    self.fence_psum()
                    evx = {}
                    for mg in range(2):
                        if mg == 0 and tgrp == 0:
                            self.fence_dma(y)
                        self.cT.wait(y)
                        wm = None
                        for ncc in range(NC22):
                            wm = self.qY.inc(y.dma_start_transpose(
                                wd[:, ncc * 512:(ncc + 1) * 512],
                                self.scr["l1d"].ap()[
                                    mg * 512:(mg + 1) * 512,
                                    ncc * 128:(ncc + 1) * 128]), 16)
                        self.qY.wait(t, wm)
                        self.cV.wait(t, hq_done_v)
                        for it in range(NTG):
                            tt = tbase + it
                            par = it % 2
                            if it >= 2:
                                self.cV.wait(t, evx[(it - 2, mg)])
                            ins = None
                            for ncc in range(NC22):
                                ins = t.matmul(
                                    psD[:, par * 512:(par + 1) * 512],
                                    hqT[:, ncc * TG + it * 128:
                                        ncc * TG + (it + 1) * 128],
                                    wd[:, ncc * 512:(ncc + 1) * 512],
                                    start=(ncc == 0), stop=(ncc == NC22 - 1))
                            mk = self.cT.inc(ins)
                            self.cT.wait(v, mk)
                            xsl = self.xr[:, tt * D + mg * 512:
                                          tt * D + (mg + 1) * 512]
                            evx[(it, mg)] = self.cV.inc(
                                v.scalar_tensor_tensor(
                                    xsl, psD[:, par * 512:(par + 1) * 512],
                                    0.5, xsl, op0=ALU.mult, op1=ALU.add))

    def emit_l1_attn(self, tag):
        nc = self.nc
        v, s, t, g, y = self.v, self.s, self.t, self.g, self.y
        st = self.stats
        H, dh = 16, 64
        mu, rr = self.emit_ln_stats()
        with (nc.sbuf_tensor(f"{tag}_xo", [128, KC * T], BF16) as xo,
              nc.sbuf_tensor(f"{tag}_kT", [128, KC * T], BF16) as kT):
            with (nc.sbuf_tensor(f"{tag}_ln", [128, 2 * D], BF16) as lnbuf,
                  nc.psum_tensor(f"{tag}_psT", [128, 2 * D], BF16) as psT):
                self.emit_xf(xo, lnbuf, psT, mu, rr)
            xf_done_v = self.cV.n
            with (nc.sbuf_tensor(f"{tag}_qT", [128, KC * T], BF16) as qT,
                  nc.sbuf_tensor(f"{tag}_vt", [128, NT * D], BF16) as vt,
                  nc.sbuf_tensor(f"{tag}_pt", [128, 2048], BF16) as ptsb,
                  nc.sbuf_tensor(f"{tag}_pb", [128, 2048], BF16) as pbsb):
                sqs = self.sqd[:, 0:128]
                trilm = self.sqd[:, 128:256]
                nrm = self.sqd[:, 256:512]
                beta = self.sqd[:, 768:1024]
                self.cV.wait(t, xf_done_v)
                self.cV.wait(g)
                self.cS.wait(g)
                g.memset(trilm, 1.0)
                self.cG.inc(g.affine_select(
                    trilm, trilm, pattern=[[-1, 128]], base=0,
                    channel_multiplier=1, compare_op=ALU.is_ge, fill=0.0))
                self.cG.wait(v)
                self.cG.wait(t)
                with nc.psum_tensor(f"{tag}_pb2", [128, 1024], F32) as psB:
                    self.fence_dma(g)
                    self.cV.wait(g)
                    g.memset(self.wb[:], 0.0)
                    self.cG.inc(g.engine_nop()) if False else None
                    wm = None
                    for kc in range(KC):
                        wm = self.qG.inc(g.dma_start(
                            out=self.wb[:, kc * 128:kc * 128 + 16],
                            in_=self.din["L1_attn_Wb"].ap()[
                                kc * 128:(kc + 1) * 128, :]), 16)
                    bbt = self.sqd[:, 0:16]
                    self.fence_dma(y)
                    bbm = self.qY.inc(y.dma_start(
                        out=bbt, in_=self.din["L1_attn_bb"].ap()[:, :]), 16)
                    self.qG.wait(t, wm)
                    self.qY.wait(v, bbm)
                    evb = {}
                    for tt in range(NT):
                        par = tt % 2
                        if tt >= 2:
                            self.cV.wait(t, evb[tt - 2])
                        ins = None
                        for kc in range(KC):
                            ins = t.matmul(
                                psB[:, par * 512:par * 512 + 128],
                                xo[:, kc * T + tt * 128:kc * T + (tt + 1) * 128],
                                self.wb[:, kc * 128:(kc + 1) * 128],
                                start=(kc == 0), stop=(kc == KC - 1))
                        mk = self.cT.inc(ins)
                        self.cT.wait(v, mk)
                        evb[tt] = self.cV.inc(v.tensor_tensor(
                            beta[:, tt * 16:(tt + 1) * 16],
                            psB[:, par * 512:par * 512 + 16],
                            bbt, op=ALU.add))

                self.cV.wait(s)
                if getattr(self, "dbg2", False):
                    self.cV.wait(g)
                    self.qG.inc(g.dma_start(out=self.dq3.ap()[:, 512:768],
                                            in_=beta), 16)
                    self.qG.inc(g.dma_start(out=self.dq3.ap()[:, 768:1024],
                                            in_=self.wb[:, 0:256]), 16)

                    self.qG.wait(g)
                    self.qG.wait(v)
                    self.qG.wait(s)
                self.cS.inc(s.activation(beta[:], beta[:], AF.Sigmoid))
                self.cS.wait(v)
                if getattr(self, "stop_phase", "") == "B1":
                    return
                for which, (W, dst, base_nrm) in enumerate(
                        (("L1_attn_Wq", qT, 0), ("L1_attn_Wk", kT, 0))):
                    with nc.psum_tensor(f"{tag}_pq{which}", [128, 1024],
                                        F32) as psQ2:
                        self.fence_psum()
                        evq = {}
                        sq_mark = {}
                        self.cV.wait(s)
                        wb_user_t = self.cT.n
                        for dc in range(KC):
                            self.cT.wait(g, wb_user_t)
                            wm = None
                            for kc in range(KC):
                                wm = self.qG.inc(g.dma_start(
                                    out=self.wb[:, kc * 128:(kc + 1) * 128],
                                    in_=self.din[W].ap()[
                                        kc * 128:(kc + 1) * 128,
                                        dc * 128:(dc + 1) * 128]), 16)
                            self.qG.wait(t, wm)
                            for tt in range(NT):
                                par = tt % 2
                                if (dc, tt) >= (0, 2):
                                    pk = (dc, tt - 2) if tt >= 2 else \
                                        (dc - 1, NT - 2 + tt)
                                    if pk in evq:
                                        self.cV.wait(t, evq[pk])
                                        self.cS.wait(t, sq_mark[pk])
                                ins = None
                                for kc in range(KC):
                                    ins = t.matmul(
                                        psQ2[:, par * 512:par * 512 + 128],
                                        xo[:, kc * T + tt * 128:
                                           kc * T + (tt + 1) * 128],
                                        self.wb[:, kc * 128:(kc + 1) * 128],
                                        start=(kc == 0), stop=(kc == KC - 1))
                                mk = self.cT.inc(ins)
                                self.cT.wait(s, mk)
                                ps = psQ2[:, par * 512:par * 512 + 128]
                                col = base_nrm + tt * 16 + dc * 2
                                s.activation(sqs[:, 0:64], ps[:, 0:64],
                                             AF.Square,
                                             accum_out=nrm[:, col:col + 1])
                                sqm = self.cS.inc(s.activation(
                                    sqs[:, 64:128], ps[:, 64:128], AF.Square,
                                    accum_out=nrm[:, col + 1:col + 2]))
                                sq_mark[(dc, tt)] = sqm
                                self.cT.wait(v, mk)
                                self.cS.wait(v, sqm)
                                evq[(dc, tt)] = self.cV.inc(v.tensor_copy(
                                    vt[:, tt * D + dc * 128:
                                       tt * D + (dc + 1) * 128], ps))
                            wb_user_t = self.cT.n
                    if getattr(self, "stop_phase", "") == "G1" and which == 0:
                        return
                    
                    
                    self.cV.wait(s)
                    self.cS.inc(s.activation(nrm[:, 0:256], nrm[:, 0:256],
                                             AF.Sqrt))
                    self.cS.wait(v)
                    v.tensor_scalar(nrm[:, 0:256], nrm[:, 0:256], 1e-12, None,
                                    op0=ALU.max)
                    self.cV.inc(v.reciprocal(nrm[:, 0:256], nrm[:, 0:256]))
                    if which == 1:
                        self.cV.inc(v.tensor_tensor(
                            nrm[:, 0:256], nrm[:, 0:256], beta[:],
                            op=ALU.mult))
                    with nc.psum_tensor(f"{tag}_pn{which}", [128, 2 * D],
                                        BF16) as psN:
                        self.fence_psum()
                        t_mark = {}
                        v_mark = {}
                        for tt in range(NT):
                            par = tt % 2
                            stg = vt[:, tt * D:(tt + 1) * D]
                            if tt >= 2:
                                self.cT.wait(v, t_mark[tt - 2])
                            rsl = nrm[:, tt * 16:(tt + 1) * 16] \
                                .unsqueeze(2).broadcast_to([128, 16, 64])
                            s3 = stg.rearrange("p (h e) -> p h e", e=64)
                            self.cV.inc(v.tensor_tensor(s3, s3, rsl,
                                                        op=ALU.mult))
                            self.cV.wait(t)
                            if tt >= 2:
                                self.cV.wait(t, v_mark[tt - 2])
                            ins = None
                            for c in range(KC):
                                ins = t.transpose(
                                    psN[:, par * D + c * 128:
                                        par * D + (c + 1) * 128],
                                    stg[:, c * 128:(c + 1) * 128],
                                    self.idn[:])
                            t_mark[tt] = self.cT.inc(ins)
                            self.cT.wait(v)
                            dstp = dst[:].rearrange(
                                "p (c tk) -> p c tk", tk=T)[
                                :, :, tt * 128:(tt + 1) * 128]
                            srcp = psN[:, par * D:(par + 1) * D].rearrange(
                                "p (c tk) -> p c tk", tk=128)
                            v_mark[tt] = self.cV.inc(v.tensor_copy(dstp, srcp))
                if getattr(self, "stop_phase", "") == "QK1":
                    return
                with nc.psum_tensor(f"{tag}_pv", [128, 1024], F32) as psV:
                    self.fence_psum()
                    evv = {}
                    wb_user_t = self.cT.n
                    for dc in range(KC):
                        self.cT.wait(g, wb_user_t)
                        wm = None
                        for kc in range(KC):
                            wm = self.qG.inc(g.dma_start(
                                out=self.wb[:, kc * 128:(kc + 1) * 128],
                                in_=self.din["L1_attn_Wv"].ap()[
                                    kc * 128:(kc + 1) * 128,
                                    dc * 128:(dc + 1) * 128]), 16)
                        self.qG.wait(t, wm)
                        for tt in range(NT):
                            par = tt % 2
                            pk = (dc, tt - 2) if tt >= 2 else \
                                (dc - 1, NT - 2 + tt)
                            if pk in evv:
                                self.cV.wait(t, evv[pk])
                            ins = None
                            for kc in range(KC):
                                ins = t.matmul(
                                    psV[:, par * 512:par * 512 + 128],
                                    xo[:, kc * T + tt * 128:
                                       kc * T + (tt + 1) * 128],
                                    self.wb[:, kc * 128:(kc + 1) * 128],
                                    start=(kc == 0), stop=(kc == KC - 1))
                            mk = self.cT.inc(ins)
                            self.cT.wait(v, mk)
                            evv[(dc, tt)] = self.cV.inc(v.tensor_copy(
                                vt[:, tt * D + dc * 128:
                                   tt * D + (dc + 1) * 128],
                                psV[:, par * 512:par * 512 + 128]))
                        wb_user_t = self.cT.n
                qkv_done_v = self.cV.n
                if getattr(self, "dbg2", False):
                    self.cV.wait(y)
                    self.cS.wait(y)
                    self.qY.inc(y.dma_start(out=self.dq1.ap()[:, 0:2048],
                                            in_=qT[:, 0:2048]), 16)
                    self.qY.inc(y.dma_start(out=self.dq2.ap()[:, 0:2048],
                                            in_=kT[:, 0:2048]), 16)
                    self.qG.inc(g.dma_start(out=self.dq3.ap()[:, 0:256],
                                            in_=beta), 16)
                    self.qG.inc(g.dma_start(out=self.dq3.ap()[:, 256:512],
                                            in_=nrm), 16)
                    self.qG.wait(y)
                    self.qY.wait(y)
                    self.qY.wait(v)
                    self.qY.wait(t)
                if getattr(self, "stop_phase", "") == "V1":
                    return
                with (nc.psum_tensor(f"{tag}_psS", [128, 2048], F32) as psS,
                      nc.psum_tensor(f"{tag}_psP", [128, 2048], BF16) as psP,
                      nc.psum_tensor(f"{tag}_psO", [128, 64], F32) as psO):
                    self.fence_psum()
                    ev_mark = 0
                    pt_mark = 0
                    evp_mark = 0
                    evo_mark = 0
                    self.cV.wait(t, qkv_done_v)
                    for h in range(H):
                        cbase = (h // 2) * T
                        pbase = (h % 2) * 64
                        for qt in range(NT):
                            nkg = qt // 4 + 1
                            nkc = qt + 1
                            self.cV.wait(t, ev_mark)
                            ins = None
                            for kg in range(nkg):
                                ins = t.matmul(
                                    psS[:, kg * 512:(kg + 1) * 512],
                                    qT[pbase:pbase + 64,
                                       cbase + qt * 128:cbase + (qt + 1) * 128],
                                    kT[pbase:pbase + 64,
                                       cbase + kg * 512:cbase + (kg + 1) * 512],
                                    start=True, stop=True)
                            smk = self.cT.inc(ins)
                            self.cT.wait(v, smk)
                            self.cT.wait(v, pt_mark)
                            if qt > 0:
                                v.tensor_copy(pbsb[:, 0:qt * 128],
                                              psS[:, 0:qt * 128])
                            ev_mark = self.cV.inc(v.tensor_tensor(
                                pbsb[:, qt * 128:(qt + 1) * 128],
                                psS[:, qt * 128:(qt + 1) * 128],
                                trilm, op=ALU.mult))
                            self.cV.wait(t, ev_mark)
                            self.cV.wait(t, evp_mark)
                            ins = None
                            for kc in range(nkc):
                                ins = t.transpose(
                                    psP[:, kc * 128:(kc + 1) * 128],
                                    pbsb[:, kc * 128:(kc + 1) * 128],
                                    self.idn[:])
                            pt_mark = self.cT.inc(ins)
                            self.cT.wait(v, pt_mark)
                            evp_mark = self.cV.inc(v.tensor_copy(
                                ptsb[:, 0:nkc * 128], psP[:, 0:nkc * 128]))
                            self.cV.wait(t, evp_mark)
                            self.cV.wait(t, evo_mark)
                            ins = None
                            for kc in range(nkc):
                                ins = t.matmul(
                                    psO[:],
                                    ptsb[:, kc * 128:(kc + 1) * 128],
                                    vt[:, kc * D + h * dh:
                                       kc * D + (h + 1) * dh],
                                    start=(kc == 0), stop=(kc == nkc - 1))
                            omk = self.cT.inc(ins)
                            self.cT.wait(v, omk)
                            col = (h // 2) * T + qt * 128 + (h % 2) * 64
                            evo_mark = self.cV.inc(v.tensor_copy(
                                xo[:, col:col + 64], psO[:]))
                o_done_v = self.cV.n
                if getattr(self, "stop_phase", "") == "SC1":
                    return
                with nc.psum_tensor(f"{tag}_psOT", [128, 2 * D], BF16) as psOT:
                    self.fence_psum()
                    self.cV.wait(t, o_done_v)
                    evq2 = {}
                    for qt in range(NT):
                        par = qt % 2
                        if qt >= 2:
                            self.cV.wait(t, evq2[qt - 2])
                        ins = None
                        for hp in range(KC):
                            ins = t.transpose(
                                psOT[:, par * D + hp * 128:
                                     par * D + (hp + 1) * 128],
                                xo[:, hp * T + qt * 128:hp * T + (qt + 1) * 128],
                                self.idn[:])
                        tmk = self.cT.inc(ins)
                        self.cT.wait(v, tmk)
                        dstp = kT[:].rearrange("p (c tk) -> p c tk", tk=T)[
                            :, :, qt * 128:(qt + 1) * 128]
                        srcp = psOT[:, par * D:(par + 1) * D].rearrange(
                            "p (c tk) -> p c tk", tk=128)
                        evq2[qt] = self.cV.inc(v.tensor_copy(dstp, srcp))
            oT_done_v = self.cV.n
            with (nc.sbuf_tensor(f"{tag}_wpj", [128, 8192], BF16) as wpj,
                  nc.psum_tensor(f"{tag}_psPr", [128, 1024], F32) as psPr):
                self.fence_dma(g)
                wm = None
                for og in range(2):
                    for dc in range(KC):
                        wm = self.qG.inc(g.dma_start(
                            out=wpj[:, og * 4096 + dc * 512:
                                    og * 4096 + (dc + 1) * 512],
                            in_=self.din["L1_attn_Wo"].ap()[
                                dc * 128:(dc + 1) * 128,
                                og * 512:(og + 1) * 512]), 16)
                self.qG.wait(t, wm)
                self.fence_psum()
                self.cV.wait(t, oT_done_v)
                evx = {}
                for tt in range(NT):
                    for og in range(2):
                        if tt >= 1:
                            self.cV.wait(t, evx[(tt - 1, og)])
                        ins = None
                        for dc in range(KC):
                            ins = t.matmul(
                                psPr[:, og * 512:(og + 1) * 512],
                                kT[:].rearrange("p (c tk) -> p c tk", tk=T)[
                                    :, dc, tt * 128:(tt + 1) * 128],
                                wpj[:, og * 4096 + dc * 512:
                                    og * 4096 + (dc + 1) * 512],
                                start=(dc == 0), stop=(dc == KC - 1))
                        mk = self.cT.inc(ins)
                        self.cT.wait(v, mk)
                        xsl = self.xr[:, tt * D + og * 512:
                                      tt * D + (og + 1) * 512]
                        evx[(tt, og)] = self.cV.inc(v.tensor_tensor(
                            xsl, xsl, psPr[:, og * 512:(og + 1) * 512],
                            op=ALU.add))

    def emit_prep_l2(self):
        nc = self.nc
        v, s, t, g, y = self.v, self.s, self.t, self.g, self.y
        st = self.stats
        with nc.sbuf_tensor("p2_w0", [128, 1024], BF16) as wsb:
            for name, scr, nrows in (("L2_attn_qkv_w", "l2qkv", 3072),
                                     ("L2_attn_proj_w", "l2proj", 1024)):
                for r in range(nrows // 128):
                    self.fence_dma(g)
                    mk = self.qG.inc(g.dma_start(
                        out=wsb[:],
                        in_=self.din[name].ap()[r * 128:(r + 1) * 128, :]), 16)
                    self.qG.wait(y, mk)
                    mk2 = self.qY.inc(y.dma_start(
                        out=self.scr[scr].ap()[r * 128:(r + 1) * 128, :],
                        in_=wsb[:]), 16)
                    self.qY.wait(g, mk2)
        with (nc.sbuf_tensor("p2_w", [128, 1024], F32) as wsb,
              nc.sbuf_tensor("p2_u", [128, 1024], F32) as usb,
              nc.sbuf_tensor("p2_a", [128, 1024], F32) as asb,
              nc.sbuf_tensor("p2_b", [128, 1024], F32) as bsb,
              nc.sbuf_tensor("p2_q", [128, 1024], BF16) as qsb,
              nc.psum_tensor("p2_pm", [128, 128], F32) as pm):
            for name, scr, nrows, ncols in (("L2_ffn_W1", "l2w1", 4096, 1024),
                                            ("L2_ffn_W2", "l2w2", 1024, 4096)):
                ntile = nrows // 128
                ncg = ncols // 1024
                acc = st[:, 226:227]
                self.cV.wait(g)
                self.gshort(g.memset(acc, 0.0))
                self.cG.wait(v)
                for r in range(ntile):
                    for cg in range(ncg):
                        self.fence_dma(y)
                        mk = self.qY.inc(y.dma_start(
                            out=wsb[:],
                            in_=self.din[name].ap()[
                                r * 128:(r + 1) * 128,
                                cg * 1024:(cg + 1) * 1024]), 16)
                        self.qY.wait(v, mk)
                        self.vshort(v.tensor_reduce(
                            st[:, 227:228], wsb[:], axis=AX.X, op=ALU.add,
                            apply_absolute_value=True))
                        self.vshort(v.tensor_tensor(acc, acc, st[:, 227:228],
                                                    op=ALU.add))
                self.part_sum(acc, st[0:1, 228:229], pm)
                self.vshort(v.tensor_copy(
                    st[0:1, 229:237], st[0:1, 228:229].broadcast_to([1, 8])))
                self.cV.wait(y)
                self.qY.inc(y.dma_start(out=self.scr["cc_in"].ap()[0:1, :],
                                        in_=st[0:1, 229:237]), 16)
                self.qY.wait(y)
                mk2 = self.qY.inc(y.dma_start(
                    out=st[:, 230:231],
                    in_=self.scr["cc_in"].ap()[0:1, 0:1]
                    .partition_broadcast(128)), 16)
                self.qY.wait(v, mk2)
                rs = st[:, 231:232]
                self.vshort(v.tensor_scalar(rs, st[:, 230:231],
                                            1.0 / (nrows * ncols), 1e-8,
                                            op0=ALU.mult, op1=ALU.max))
                self.vshort(v.reciprocal(rs, rs))
                for r in range(ntile):
                    for cg in range(ncg):
                        self.fence_dma(y)
                        mk = self.qY.inc(y.dma_start(
                            out=wsb[:],
                            in_=self.din[name].ap()[
                                r * 128:(r + 1) * 128,
                                cg * 1024:(cg + 1) * 1024]), 16)
                        self.qY.wait(v, mk)
                        v.tensor_scalar(usb[:], wsb[:], rs, None, op0=ALU.mult)
                        v.tensor_scalar(asb[:], usb[:], -1.0, None,
                                        op0=ALU.mult)
                        v.tensor_tensor(asb[:], usb[:], asb[:], op=ALU.max)
                        v.tensor_scalar(bsb[:], asb[:], 0.5, None,
                                        op0=ALU.is_ge)
                        v.scalar_tensor_tensor(bsb[:], asb[:], 1.5, bsb[:],
                                               op0=ALU.is_ge, op1=ALU.add)
                        v.tensor_scalar(bsb[:], bsb[:], 1.0, None, op0=ALU.min)
                        v.tensor_scalar(asb[:], usb[:], 0.0, None,
                                        op0=ALU.is_gt)
                        v.tensor_scalar(usb[:], usb[:], 0.0, None,
                                        op0=ALU.is_lt)
                        v.tensor_tensor(asb[:], asb[:], usb[:],
                                        op=ALU.subtract)
                        self.cV.inc(v.tensor_tensor(qsb[:], bsb[:], asb[:],
                                                    op=ALU.mult))
                        self.cV.wait(y)
                        self.qY.inc(y.dma_start(
                            out=self.scr[scr].ap()[
                                r * 128:(r + 1) * 128,
                                cg * 1024:(cg + 1) * 1024],
                            in_=qsb[:]), 16)
                        self.qY.wait(v)
        with (nc.sbuf_tensor("p2b_w", [128, 1024], F32) as wsb,
              nc.sbuf_tensor("p2b_t", [128, 16], F32) as tab,
              nc.sbuf_tensor("p2b_c", [128, 16], F32) as cst,
              nc.psum_tensor("p2b_ps", [128, 16], F32) as psC):
            for r in range(16):
                r0 = r * 128
                rn = min(2047 - r0, 128)
                self.fence_dma(y)
                mk = self.qY.inc(y.dma_start(
                    out=wsb[0:rn, :],
                    in_=self.din["L2_attn_rel_pos_emb"].ap()[r0:r0 + rn, :]),
                    16)
                self.qY.wait(v, mk)
                self.vshort(v.tensor_reduce(
                    tab[0:rn, :],
                    wsb[0:rn, :].rearrange("p (h e) -> p h e", e=64),
                    axis=AX.X, op=ALU.add))
                self.cV.wait(y)
                if r0 + rn > 896:
                    p0 = max(0, 896 - r0)
                    c0 = 128 + (r0 + p0 - 1023)
                    dst = bass.AP(self.scr["l2bias"], c0,
                                  [[1, rn - p0], [2304, 16]])
                    self.qY.inc(y.dma_start(out=dst, in_=tab[p0:rn, 0:16],
                                            allow_slow_non_contiguous=True),
                                16)
                    self.qY.wait(y)
                if r == 15:
                    self.cV.wait(g)
                    self.cS.wait(g)
                    self.gshort(g.memset(self.sqd[0:1, 0:128], 1.0))
                    self.cV.wait(y)
                    rowm = self.qY.inc(y.dma_start(out=cst[0:1, 0:16],
                                                   in_=tab[126:127, 0:16]),
                                       16)
                    self.cG.wait(t)
                    self.cV.wait(t)
                    self.qY.wait(t, rowm)
                    tmk = self.cT.inc(t.matmul(
                        psC[:, 0:16], self.sqd[0:1, 0:128],
                        cst[0:1, 0:16], start=True, stop=True))
                    self.cT.wait(v, tmk)
                    self.vshort(v.tensor_copy(cst[:], psC[:, 0:16]))
                    self.cV.wait(y)
                    for kb in range(9):
                        c0 = 1152 + kb * 128
                        dst = bass.AP(self.scr["l2bias"], c0,
                                      [[1, 128], [2304, 16]])
                        self.qY.inc(y.dma_start(
                            out=dst, in_=cst[:, 0:16],
                            allow_slow_non_contiguous=True), 16)
                    self.qY.wait(y)

    def part_sum(self, col_ap, out_row_ap, pm):
        self.cV.wait(self.t)
        self.cT.inc(self.t.transpose(pm[0:1, 0:128], col_ap, self.idn_f32[:]))
        self.cT.wait(self.v)
        self.vshort(self.v.tensor_reduce(out_row_ap, pm[0:1, 0:128],
                                         axis=AX.X, op=ALU.add))

    def emit_l2_ffn(self, tag):
        nc = self.nc
        v, s, t, g, y = self.v, self.s, self.t, self.g, self.y
        FF = 4096
        NC32 = FF // 128
        TG = 512
        NTG = 4
        mu, rr = self.emit_ln_stats()
        with (nc.sbuf_tensor(f"{tag}_xf", [128, KC * TG], BF16) as xf,
              nc.sbuf_tensor(f"{tag}_hT", [128, NC32 * TG], BF16) as hT,
              nc.sbuf_tensor(f"{tag}_b2", [128, D], F32) as b2bc):
            self.cV.wait(y)
            self.cS.wait(y)
            b2m = self.bcast_load(b2bc[:], "L2_ffn_b2", D)
            for tgrp in range(4):
                tiles = range(tgrp * NTG, (tgrp + 1) * NTG)
                with (nc.sbuf_tensor(f"{tag}{tgrp}_ln", [128, 2 * D], BF16)
                      as lnbuf,
                      nc.psum_tensor(f"{tag}{tgrp}_psT", [128, 2 * D], BF16)
                      as psT):
                    self.fence_psum()
                    self.emit_xf(xf, lnbuf, psT, mu, rr, tiles=tiles, tk=TG)
                xf_done_v = self.cV.n
                with (nc.sbuf_tensor(f"{tag}{tgrp}_wg", [128, KC * 512], BF16)
                      as wg,
                      nc.sbuf_tensor(f"{tag}{tgrp}_b1", [128, 512], F32)
                      as b1bc,
                      nc.sbuf_tensor(f"{tag}{tgrp}_hf", [128, 512], F32) as hf,
                      nc.sbuf_tensor(f"{tag}{tgrp}_hq", [128, 2 * 512], BF16)
                      as hqb,
                      nc.psum_tensor(f"{tag}{tgrp}_pw", [128, 2 * 512], F32)
                      as psW,
                      nc.psum_tensor(f"{tag}{tgrp}_ph", [128, 2 * 1024], BF16)
                      as psH):
                    self.fence_psum()
                    self.cV.wait(t, xf_done_v)
                    thq = {}
                    gi = 0
                    for ng in range(8):
                        n0 = ng * 512
                        self.cT.wait(y)
                        self.cV.wait(y)
                        wm = None
                        for kc in range(KC):
                            wm = self.qY.inc(y.dma_start_transpose(
                                wg[:, kc * 512:(kc + 1) * 512],
                                self.scr["l2w1"].ap()[
                                    n0:n0 + 512, kc * 128:(kc + 1) * 128]), 16)
                        bm = self.bcast_load(b1bc[:], "L2_ffn_b1", 512,
                                             col0=n0)
                        self.qY.wait(t, wm)
                        self.qY.wait(v, bm)
                        for it in range(NTG):
                            gi += 1
                            par = gi % 2
                            if gi > 2:
                                self.cV.wait(t, thq.get((gi - 2, "v"), 0))
                            ins = None
                            for kc in range(KC):
                                ins = t.matmul(
                                    psW[:, par * 512:(par + 1) * 512],
                                    xf[:, kc * TG + it * 128:
                                       kc * TG + (it + 1) * 128],
                                    wg[:, kc * 512:(kc + 1) * 512],
                                    start=(kc == 0), stop=(kc == KC - 1))
                            mk = self.cT.inc(ins)
                            self.cT.wait(v, mk)
                            if gi > 1:
                                self.cS.wait(v, thq.get((gi - 1, "s"), 0))
                            self.cV.inc(v.tensor_tensor(
                                hf[:], psW[:, par * 512:(par + 1) * 512],
                                b1bc[:], op=ALU.add))
                            self.cV.wait(s)
                            if gi > 2:
                                self.cT.wait(s, thq.get((gi - 2, "t"), 0))
                            smk = self.cS.inc(s.activation(
                                hqb[:, par * 512:(par + 1) * 512], hf[:],
                                AF.Gelu))
                            thq[(gi, "s")] = smk
                            self.cS.wait(t, smk)
                            psl = par * 1024
                            ins = None
                            for c in range(4):
                                ins = t.transpose(
                                    psH[:, psl + c * 128:psl + (c + 1) * 128],
                                    hqb[:, par * 512 + c * 128:
                                        par * 512 + (c + 1) * 128],
                                    self.idn[:])
                            tm = self.cT.inc(ins)
                            thq[(gi, "t")] = tm
                            self.cT.wait(v, tm)
                            dst = hT[:].rearrange("p (c tk) -> p c tk",
                                                  tk=TG)[
                                :, 4 * ng:4 * ng + 4, it * 128:(it + 1) * 128]
                            srcp = psH[:, psl:psl + 512].rearrange(
                                "p (c tk) -> p c tk", tk=128)
                            thq[(gi, "v")] = self.cV.inc(
                                v.tensor_copy(dst, srcp))
                hT_done_v = self.cV.n
                with (nc.sbuf_tensor(f"{tag}{tgrp}_wd", [128, 16 * 512], BF16)
                      as wd,
                      nc.psum_tensor(f"{tag}{tgrp}_pd", [128, 2 * 512], F32)
                      as psD):
                    self.fence_psum()
                    self.qY.wait(v, b2m)
                    evx = {}
                    for mg in range(2):
                        for it in range(NTG):
                            tt = tgrp * NTG + it
                            par = it % 2
                            ins = None
                            for half in range(2):
                                self.cT.wait(y)
                                wm = None
                                for ncc in range(16):
                                    nci = half * 16 + ncc
                                    wm = self.qY.inc(y.dma_start_transpose(
                                        wd[:, ncc * 512:(ncc + 1) * 512],
                                        self.scr["l2w2"].ap()[
                                            mg * 512:(mg + 1) * 512,
                                            nci * 128:(nci + 1) * 128]), 16)
                                self.qY.wait(t, wm)
                                if half == 0 and it >= 2:
                                    self.cV.wait(t, evx[(it - 2, mg)])
                                self.cV.wait(t, hT_done_v)
                                for ncc in range(16):
                                    nci = half * 16 + ncc
                                    ins = t.matmul(
                                        psD[:, par * 512:(par + 1) * 512],
                                        hT[:, nci * TG + it * 128:
                                           nci * TG + (it + 1) * 128],
                                        wd[:, ncc * 512:(ncc + 1) * 512],
                                        start=(nci == 0), stop=(nci == 31))
                                ins = self.nc.tensor.nop() if False else ins
                                self.cT.inc(ins)
                                ins = None
                            mk = self.cT.n
                            self.cT.wait(v, mk)
                            xsl = self.xr[:, tt * D + mg * 512:
                                          tt * D + (mg + 1) * 512]
                            v.tensor_tensor(
                                psD[:, par * 512:(par + 1) * 512],
                                psD[:, par * 512:(par + 1) * 512],
                                b2bc[:, mg * 512:(mg + 1) * 512], op=ALU.add)
                            evx[(it, mg)] = self.cV.inc(
                                v.scalar_tensor_tensor(
                                    xsl, psD[:, par * 512:(par + 1) * 512],
                                    0.5, xsl, op0=ALU.mult, op1=ALU.add))

    def emit_l2_attn(self, tag):
        nc = self.nc
        v, s, t, g, y = self.v, self.s, self.t, self.g, self.y
        st = self.stats
        H, dh = 16, 64
        mu, rr = self.emit_ln_stats()
        with (nc.sbuf_tensor(f"{tag}_xo", [128, KC * T], BF16) as xo,
              nc.sbuf_tensor(f"{tag}_kT", [128, KC * T], BF16) as kT):
            with (nc.sbuf_tensor(f"{tag}_ln", [128, 2 * D], BF16) as lnbuf,
                  nc.psum_tensor(f"{tag}_psT", [128, 2 * D], BF16) as psT):
                self.fence_psum()
                self.emit_xf(xo, lnbuf, psT, mu, rr)
            xf_done_v = self.cV.n
            with (nc.sbuf_tensor(f"{tag}_qT", [128, KC * T], BF16) as qT,
                  nc.sbuf_tensor(f"{tag}_vt", [128, NT * D], BF16) as vt,
                  nc.sbuf_tensor(f"{tag}_pt", [128, 2048], BF16) as ptsb,
                  nc.sbuf_tensor(f"{tag}_pb", [128, 2048], BF16) as pbsb):
                bwlo = self.sqd[:, 0:1024]
                bwmid = self.wb[:, 0:896]
                camask = self.wb[:, 896:1024]
                qkb = st[:, 226:242]
                self.cV.wait(t, xf_done_v)
                self.fence_dma(y)
                bqm = self.qY.inc(y.dma_start(
                    out=qkb,
                    in_=self.din["L2_attn_qkv_b"].ap()[0:1, 0:2048]
                    .rearrange("o (c p) -> p (o c)", p=128),
                    allow_slow_non_contiguous=True), 16)
                self.fence_dma(g)
                vbm = self.qG.inc(g.dma_start(
                    out=self.sqd[:, 512:1024],
                    in_=self.din["L2_attn_qkv_b"].ap()[0:1, 2048:2560]
                    .partition_broadcast(128)), 16)
                vbm2 = self.qG.inc(g.dma_start(
                    out=self.sqd[:, 0:512],
                    in_=self.din["L2_attn_qkv_b"].ap()[0:1, 2560:3072]
                    .partition_broadcast(128)), 16)
                with nc.psum_tensor(f"{tag}_pv", [128, 1024], F32) as psV:
                    self.fence_psum()
                    ev = {}
                    wb_user_t = self.cT.n
                    for og in range(2):
                        self.cT.wait(y)
                        wm = None
                        for kc in range(KC):
                            wm = self.qY.inc(y.dma_start_transpose(
                                kT[:, (kc * 2 + og) * 512:
                                   (kc * 2 + og + 1) * 512],
                                self.scr["l2qkv"].ap()[
                                    2048 + og * 512:2048 + (og + 1) * 512,
                                    kc * 128:(kc + 1) * 128]), 16)
                    self.qY.wait(t)
                    self.qG.wait(v, vbm)
                    self.qG.wait(v, vbm2)
                    for tt in range(NT):
                        for og in range(2):
                            if tt >= 1:
                                self.cV.wait(t, ev[(tt - 1, og)])
                            ins = None
                            for kc in range(KC):
                                ins = t.matmul(
                                    psV[:, og * 512:(og + 1) * 512],
                                    xo[:, kc * T + tt * 128:
                                       kc * T + (tt + 1) * 128],
                                    kT[:, (kc * 2 + og) * 512:
                                       (kc * 2 + og + 1) * 512],
                                    start=(kc == 0), stop=(kc == KC - 1))
                            mk = self.cT.inc(ins)
                            self.cT.wait(v, mk)
                            vbs = self.sqd[:, 512:1024] if og == 0 else \
                                self.sqd[:, 0:512]
                            ev[(tt, og)] = self.cV.inc(v.tensor_tensor(
                                vt[:, tt * D + og * 512:
                                   tt * D + (og + 1) * 512],
                                psV[:, og * 512:(og + 1) * 512],
                                vbs, op=ALU.add))
                v_done_t = self.cT.n
                self.qY.wait(v, bqm)
                with nc.psum_tensor(f"{tag}_pq", [128, 1024], F32) as psQK:
                    self.fence_psum()
                    evp = [0, 0]
                    gi = 0
                    wb_user_t = v_done_t
                    for qq, (row0, dst, scale, bcol) in enumerate(
                            ((0, qT, 0.125, 0), (1024, kT, 1.0, 8))):
                        for dc in range(KC):
                            self.cT.wait(y, wb_user_t)
                            wm = None
                            for kc in range(KC):
                                wm = self.qY.inc(y.dma_start_transpose(
                                    self.wb[:, kc * 128:(kc + 1) * 128],
                                    self.scr["l2qkv"].ap()[
                                        row0 + dc * 128:row0 + (dc + 1) * 128,
                                        kc * 128:(kc + 1) * 128]), 16)
                            self.qY.wait(t, wm)
                            for tg in range(4):
                                par = gi % 2
                                gi += 1
                                self.cV.wait(t, evp[par])
                                ins = None
                                for kc in range(KC):
                                    ins = t.matmul(
                                        psQK[:, par * 512:(par + 1) * 512],
                                        self.wb[:, kc * 128:(kc + 1) * 128],
                                        xo[:, kc * T + tg * 512:
                                           kc * T + (tg + 1) * 512],
                                        start=(kc == 0), stop=(kc == KC - 1))
                                mk = self.cT.inc(ins)
                                self.cT.wait(v, mk)
                                evp[par] = self.cV.inc(v.tensor_scalar(
                                    dst[:, dc * T + tg * 512:
                                        dc * T + (tg + 1) * 512],
                                    psQK[:, par * 512:(par + 1) * 512],
                                    qkb[:, bcol + dc:bcol + dc + 1],
                                    scale, op0=ALU.add, op1=ALU.mult))
                            wb_user_t = self.cT.n
                qkv_done_v = self.cV.n
                qk_done_t = self.cT.n
                with (nc.psum_tensor(f"{tag}_psS", [128, 2048], F32) as psS,
                      nc.psum_tensor(f"{tag}_psP", [128, 2048], BF16) as psP,
                      nc.psum_tensor(f"{tag}_psO", [128, 64], F32) as psO):
                    self.fence_psum()
                    exp_mark = 0
                    pt_mark = 0
                    evp_mark = 0
                    evo_mark = 0
                    mask_mark = 0
                    self.cV.wait(t, qkv_done_v)
                    self.cT.wait(g, qk_done_t)
                    g.memset(camask, 0.0)
                    self.cG.inc(g.affine_select(
                        camask, camask, pattern=[[-1, 128]], base=0,
                        channel_multiplier=1, compare_op=ALU.is_ge,
                        fill=-30000.0))
                    self.cG.wait(v)
                    for h in range(H):
                        cbase = (h // 2) * T
                        pbase = (h % 2) * 64
                        self.cV.wait(y, mask_mark)
                        self.cV.wait(g, mask_mark)
                        if h == 0:
                            self.cV.wait(y, qkv_done_v)
                        bl_lo = self.qY.inc(y.dma_start(
                            out=bwlo,
                            in_=bass.AP(self.scr["l2bias"], h * 2304 + 1,
                                        [[1, 128], [1, 1024]])), 16)
                        bl_hi = self.qG.inc(g.dma_start(
                            out=bwmid,
                            in_=bass.AP(self.scr["l2bias"], h * 2304 + 1025,
                                        [[1, 128], [1, 896]])), 16)
                        self.qY.wait(v, bl_lo)
                        self.qG.wait(v, bl_hi)
                        for qt in range(NT):
                            nkg = qt // 4 + 1
                            nkc = qt + 1
                            W = nkc * 128
                            self.cS.wait(t, exp_mark)
                            ins = None
                            for kg in range(nkg):
                                ins = t.matmul(
                                    psS[:, kg * 512:(kg + 1) * 512],
                                    qT[pbase:pbase + 64,
                                       cbase + qt * 128:cbase + (qt + 1) * 128],
                                    kT[pbase:pbase + 64,
                                       cbase + kg * 512:cbase + (kg + 1) * 512],
                                    start=True, stop=True)
                            smk = self.cT.inc(ins)
                            self.cT.wait(v, smk)
                            if W <= 1024:
                                v.tensor_tensor(
                                    psS[:, 0:W], psS[:, 0:W],
                                    self.sqd[:, 0:W][:, ::-1], op=ALU.add)
                            else:
                                v.tensor_tensor(
                                    psS[:, W - 1024:W], psS[:, W - 1024:W],
                                    self.sqd[:, 0:1024][:, ::-1], op=ALU.add)
                                M = min(W - 1024, 896)
                                v.tensor_tensor(
                                    psS[:, W - 1024 - M:W - 1024],
                                    psS[:, W - 1024 - M:W - 1024],
                                    self.wb[:, 0:M][:, ::-1], op=ALU.add)
                                if W - 1024 - M > 0:
                                    v.tensor_tensor(
                                        psS[:, 0:W - 1024 - M],
                                        psS[:, 0:W - 1024 - M],
                                        self.wb[:, 895:896].broadcast_to(
                                            [128, W - 1024 - M]),
                                        op=ALU.add)
                            mask_mark = self.cV.inc(v.tensor_tensor(
                                psS[:, qt * 128:(qt + 1) * 128],
                                psS[:, qt * 128:(qt + 1) * 128],
                                camask, op=ALU.add))
                            self.vshort(v.tensor_reduce(
                                st[:, 200:201], psS[:, 0:W], axis=AX.X,
                                op=ALU.max, negate=True))
                            self.cV.wait(s)
                            self.cT.wait(s, pt_mark)
                            exp_mark = self.cS.inc(s.activation(
                                pbsb[:, 0:W], psS[:, 0:W], AF.Exp,
                                bias=st[:, 200:201], scale=1.0,
                                accum_out=st[:, 201:202]))
                            self.cS.wait(v)
                            self.vshort(v.reciprocal(st[:, 202:203],
                                                     st[:, 201:202]))
                            self.cV.wait(t, evp_mark)
                            self.cS.wait(t)
                            ins = None
                            for kc in range(nkc):
                                ins = t.transpose(
                                    psP[:, kc * 128:(kc + 1) * 128],
                                    pbsb[:, kc * 128:(kc + 1) * 128],
                                    self.idn[:])
                            pt_mark = self.cT.inc(ins)
                            self.cT.wait(v, pt_mark)
                            evp_mark = self.cV.inc(v.tensor_copy(
                                ptsb[:, 0:W], psP[:, 0:W]))
                            self.cV.wait(t, evp_mark)
                            self.cV.wait(t, evo_mark)
                            ins = None
                            for kc in range(nkc):
                                ins = t.matmul(
                                    psO[:],
                                    ptsb[:, kc * 128:(kc + 1) * 128],
                                    vt[:, kc * D + h * dh:
                                       kc * D + (h + 1) * dh],
                                    start=(kc == 0), stop=(kc == nkc - 1))
                            omk = self.cT.inc(ins)
                            self.cT.wait(v, omk)
                            col = (h // 2) * T + qt * 128 + (h % 2) * 64
                            evo_mark = self.cV.inc(v.tensor_scalar(
                                xo[:, col:col + 64],
                                psO[:], st[:, 202:203], None, op0=ALU.mult))
                o_done_v = self.cV.n
                with nc.psum_tensor(f"{tag}_psOT", [128, 2 * D], BF16) as psOT:
                    self.fence_psum()
                    self.cV.wait(t, o_done_v)
                    evq = {}
                    for qt in range(NT):
                        par = qt % 2
                        if qt >= 2:
                            self.cV.wait(t, evq[qt - 2])
                        ins = None
                        for hp in range(KC):
                            ins = t.transpose(
                                psOT[:, par * D + hp * 128:
                                     par * D + (hp + 1) * 128],
                                xo[:, hp * T + qt * 128:hp * T + (qt + 1) * 128],
                                self.idn[:])
                        tmk = self.cT.inc(ins)
                        self.cT.wait(v, tmk)
                        dstp = kT[:].rearrange("p (c tk) -> p c tk", tk=T)[
                            :, :, qt * 128:(qt + 1) * 128]
                        srcp = psOT[:, par * D:(par + 1) * D].rearrange(
                            "p (c tk) -> p c tk", tk=128)
                        evq[qt] = self.cV.inc(v.tensor_copy(dstp, srcp))
            oT_done_v = self.cV.n
            with (nc.sbuf_tensor(f"{tag}_wpj", [128, 8192], BF16) as wpj,
                  nc.sbuf_tensor(f"{tag}_cb", [128, D], F32) as cbl,
                  nc.psum_tensor(f"{tag}_psPr", [128, 1024], F32) as psPr):
                self.fence_psum()
                self.fence_dma(y)
                b1 = self.bcast_load(cbl[:, 0:D], "L2_attn_proj_b", D)
                wm = None
                for og in range(2):
                    for dc in range(KC):
                        wm = self.qY.inc(y.dma_start_transpose(
                            wpj[:, og * 4096 + dc * 512:
                                og * 4096 + (dc + 1) * 512],
                            self.scr["l2proj"].ap()[
                                og * 512:(og + 1) * 512,
                                dc * 128:(dc + 1) * 128]), 16)
                self.qY.wait(v, b1)
                self.qY.wait(t, wm)
                self.cV.wait(t, oT_done_v)
                evx = {}
                for tt in range(NT):
                    for og in range(2):
                        if tt >= 1:
                            self.cV.wait(t, evx[(tt - 1, og)])
                        ins = None
                        for dc in range(KC):
                            ins = t.matmul(
                                psPr[:, og * 512:(og + 1) * 512],
                                kT[:].rearrange("p (c tk) -> p c tk", tk=T)[
                                    :, dc, tt * 128:(tt + 1) * 128],
                                wpj[:, og * 4096 + dc * 512:
                                    og * 4096 + (dc + 1) * 512],
                                start=(dc == 0), stop=(dc == KC - 1))
                        mk = self.cT.inc(ins)
                        self.cT.wait(v, mk)
                        xsl = self.xr[:, tt * D + og * 512:
                                      tt * D + (og + 1) * 512]
                        v.tensor_tensor(psPr[:, og * 512:(og + 1) * 512],
                                        psPr[:, og * 512:(og + 1) * 512],
                                        cbl[:, og * 512:(og + 1) * 512],
                                        op=ALU.add)
                        evx[(tt, og)] = self.cV.inc(v.tensor_tensor(
                            xsl, xsl, psPr[:, og * 512:(og + 1) * 512],
                            op=ALU.add))

    def build(self):
        nc = self.nc
        if getattr(self, "dbg2", False):
            self.dq1 = nc.dram_tensor("dq1", [128, 4096], BF16,
                                      kind="ExternalOutput")
            self.dq2 = nc.dram_tensor("dq2", [128, 4096], BF16,
                                      kind="ExternalOutput")
            self.dq3 = nc.dram_tensor("dq3", [128, 1024], BF16,
                                      kind="ExternalOutput")
        if getattr(self, "dbg", False):
            self.ddbg = nc.dram_tensor("dbg", [128, 256], F32,
                                       kind="ExternalOutput")
            self.ddbg2 = nc.dram_tensor("dbg2", [128, 2048], BF16,
                                        kind="ExternalOutput")
            self.ddbg3 = nc.dram_tensor("dbg3", [128, 2048], BF16,
                                        kind="ExternalOutput")
        with nc.Block() as blk:
            self.emit_setup()
            if self.n_sub >= 1:
                self.emit_prep_l0w()
                self.emit_l0_ffn("f0a")
            if self.n_sub >= 2:
                self.emit_l0_attn("a0")
            if self.n_sub >= 3:
                self.emit_l0_ffn("f0b")
            if self.n_sub >= 4:
                self.emit_prep_l1()
                self.emit_l1_ffn("f1a")
            if self.n_sub >= 5:
                self.emit_l1_attn("a1")
            if self.n_sub >= 6:
                self.emit_l1_ffn("f1b")
            if self.n_sub >= 7:
                self.emit_prep_l2()
                self.emit_l2_ffn("f2a")
            if self.n_sub >= 8:
                self.emit_l2_attn("a2")
            if self.n_sub >= 9:
                self.emit_l2_ffn("f2b")
            if getattr(self, "dbg", False):
                self.cV.wait(self.y)
                self.cS.wait(self.y)
                self.qY.inc(self.y.dma_start(out=self.ddbg.ap()[:, :],
                                             in_=self.stats[:]), 16)
            self.cV.wait(self.y)
            src = self.xr[:].rearrange("p (n d) -> p n d", d=D)
            dst = self.dout.ap().rearrange("(n p) d -> p n d", p=128)
            self.qY.inc(self.y.dma_start(out=dst, in_=src), 16)
            self.qY.wait(self.y)
        return nc


def build_program(n_sub=12):
    return KB(n_sub).build()


def make_in_maps(inputs):
    x = np.ascontiguousarray(np.asarray(inputs["L0_attn_x"], dtype=np.float32))
    in_maps = []
    for i in range(8):
        m = {"x": np.ascontiguousarray(x[i])}
        for k, vv in inputs.items():
            if k == "L0_attn_x":
                continue
            a = np.asarray(vv, dtype=np.float32)
            if k == "L3_ffn_W1":
                a = a.reshape(8 * 4096, 1024)
            elif k == "L3_ffn_W2":
                a = a.reshape(8 * 1024, 4096)
            elif a.ndim == 1:
                a = a[None, :]
            if k == "L1_attn_bb":
                a = np.broadcast_to(a.reshape(1, 16), (128, 16))
            m[k] = np.ascontiguousarray(a)
        in_maps.append(m)
    return in_maps


def kernel(**inputs):
    nc = build_program(12)
    in_maps = make_in_maps(inputs)
    res = run_bass_kernel_spmd(nc, in_maps, core_ids=list(range(8)))
    return np.stack([res.results[i]["out"] for i in range(8)], axis=0)
